# version 23
# baseline (speedup 1.0000x reference)
import numpy as np
import ml_dtypes
import contextlib
import concourse.bass as bass
import concourse.mybir as mybir
from concourse.bass_utils import run_bass_kernel_spmd

F32 = mybir.dt.float32
BF16 = mybir.dt.bfloat16
AF = mybir.ActivationFunctionType
ALU = mybir.AluOpType
AX = mybir.AxisListType
BFH = ml_dtypes.bfloat16

TH = 1024
NCH = 16
NPAIR = 16
NQ = 16
NST = 16
EPS = 1e-5
NCORE = 8

XN2_O = 0
SLABF_O = 0
SLABQ_O = 8192
WSLAB_O = 16384
QBF_O = 20480
DUMPS_O = 20480
WCAT_O = 22528


def WCAT_C(d):
    return WCAT_O + 2048 * d


TABC_O = 55296
FINB_O = 55296
KBF_O = 59392
VBF_O = 60416
KROT_O = 61440
VROT_O = 62464
KSQ_O = 63488
T1B_O = 64512
NK2_O = 65536
NV2_O = 66560
PBUF_O = 67584
PRET_O = 69632
WBB_O = 71680
M1B_O = 72704
MB_O = 73728
ADUMP_O = KBF_O
PSW_O = 74752
ONESB_O = 74880
BF_TOT = 88064

XST_O = 0
NRMA_O = 0
NRMB_O = 1024
AROW_O = 0
BROW_O = 1024
X2R_O = 0
XTB_O = 2048
OUTR_O = 2048
FFWR_O = 4096
BIAST_O = 4096
COLS_O = 6144


def COL(i):
    return COLS_O + 64 * i


(C_S1A, C_S1B, C_S2A, C_S2B, C_MU, C_E2, C_VAR1, C_A1, C_RA, C_RS, C_VAR2,
 C_B2, C_RS2, C_ALPHA, C_BETA, C_C1, C_C2, C_T1, C_T2, C_S1S, C_S2S) = range(21)
MISC_O = COLS_O + 21 * 64
ABTIN_O = MISC_O
ABTEV_O = MISC_O + 128
BT16_O = MISC_O + 256
THR_O = MISC_O + 384
APART_O = MISC_O + 448
A2_O = MISC_O + 480
ACOL_O = MISC_O + 496
FA_O = MISC_O + 512
ONESF_O = MISC_O + 576
PAR_O = MISC_O + 704
GCOL_O = PAR_O
BCOL_O = PAR_O + 16
LNG_O = PAR_O + 32
LNB_O = PAR_O + 48
BIN_O = PAR_O + 64
BOUTC_O = PAR_O + 112
FBC_O = PAR_O + 128
ONES1_O = PAR_O + 144
IDENT_O = PAR_O + 272
F32_TOT = PAR_O + 400

BARS = ("b_sy", "b_gp", "b_ac", "b_ve", "b_pe")


def build_nc():
    nc = bass.Bass("TRN2", target_bir_lowering=False, detect_race_conditions=False)

    x_d = nc.dram_tensor("x", [2048, 2048], F32, kind="ExternalInput").ap()
    xt_d = nc.dram_tensor("xt", [NCH, 128, 2048], F32, kind="ExternalInput").ap()
    win_d = nc.dram_tensor("win", [48, 128, NCH, 128], F32, kind="ExternalInput").ap()
    wout_d = nc.dram_tensor("wout", [16, 128, 2048], F32, kind="ExternalInput").ap()
    ffw_d = nc.dram_tensor("ffw", [NCH, 128, 2048], F32, kind="ExternalInput").ap()
    cost_d = nc.dram_tensor("cost", [2, 8, 128, TH], BF16, kind="ExternalInput").ap()
    sint_d = nc.dram_tensor("sint", [2, 8, 128, TH], BF16, kind="ExternalInput").ap()
    gcol_d = nc.dram_tensor("gcol", [128, 16], F32, kind="ExternalInput").ap()
    bcol_d = nc.dram_tensor("bcol", [128, 16], F32, kind="ExternalInput").ap()
    lng_d = nc.dram_tensor("lngcol", [128, 16], F32, kind="ExternalInput").ap()
    lnb_d = nc.dram_tensor("lnbcol", [128, 16], F32, kind="ExternalInput").ap()
    bin_d = nc.dram_tensor("bincol", [128, 48], F32, kind="ExternalInput").ap()
    boutr_d = nc.dram_tensor("boutrow", [1, 2048], F32, kind="ExternalInput").ap()
    fbr_d = nc.dram_tensor("fbrow", [1, 2048], F32, kind="ExternalInput").ap()
    psw_d = nc.dram_tensor("pairswap", [128, 128], BF16, kind="ExternalInput").ap()
    armask_d = nc.dram_tensor("armask", [128, 64], F32, kind="ExternalInput").ap()
    onesb_d = nc.dram_tensor("onesb", [128, 128], BF16, kind="ExternalInput").ap()
    ident_d = nc.dram_tensor("ident", [128, 128], F32, kind="ExternalInput").ap()
    out_d = nc.dram_tensor("out", [2048, 2048], F32, kind="ExternalOutput").ap()

    qh = [nc.dram_tensor(f"qh{p}", [8, 128, NCH, 128], BF16).ap() for p in range(2)]
    fh = [nc.dram_tensor(f"fh{p}", [8, 128, NCH, 128], BF16).ap() for p in range(2)]
    ph = [nc.dram_tensor(f"ph{p}", [NPAIR, 128, TH], BF16).ap() for p in range(2)]
    abd = nc.dram_tensor("abd", [16, 128], F32).ap()
    ar_in = nc.dram_tensor("ar_in", [128, 64], F32).ap()
    ar_out = nc.dram_tensor("ar_out", [128, 64], F32, addr_space="Shared").ap()

    es = contextlib.ExitStack()
    hb = es.enter_context(nc.sbuf_tensor("hb", [128, BF_TOT], BF16))
    hf = es.enter_context(nc.sbuf_tensor("hf", [128, F32_TOT], F32))
    psA = es.enter_context(nc.psum_tensor("psA", [128, 2048], F32))
    psB = es.enter_context(nc.psum_tensor("psB", [128, 2048], F32))

    SEMS = list(BARS) + [
        "sm_par", "sm_x", "sm_tab", "sm_fe", "sm_qe", "sm_x2", "sm_sr", "sm_out",
        "gm_xt", "gm_ff", "gm_w", "gm_pe", "gm_pr", "gm_wc", "gm_misc", "cc",
        "am_st", "am_sq", "am_aff", "am_ffq", "am_evac", "am_ksq", "am_sp",
        "pm_mm", "pm_rh", "pm_nm", "pm_bc", "pm_ph2", "vm"]
    S = {n: es.enter_context(nc.semaphore(n)) for n in SEMS}

    M = {}
    vmc = [0]

    def vinc(ins, key=None):
        ins.then_inc(S["vm"], 1)
        vmc[0] += 1
        if key is not None:
            M[key] = vmc[0]
        return vmc[0]

    def B2(off, n):
        return hb[:, off:off + n]

    def B3(off, c, n):
        return hb[:, off:off + c * n].rearrange("p (c n) -> p c n", n=n)

    def Fv(off, n):
        return hf[:, off:off + n]

    def F_mm(p, k):
        return 48 * p + k

    def F_rh(p, k):
        return 16 * p + k

    def F_evac(p, k):
        return 48 * p + k

    def F_ksq(p, k):
        return 32 * p + k

    def F_aff(p, k):
        return 32 * p + k

    def tab_cos(j):
        return B2(TABC_O + (j % 2) * 2048, 1024)

    def tab_sin(j):
        return B2(TABC_O + (j % 2) * 2048 + 1024, 1024)

    def bar(eng, own, k):
        eng.sem_inc(S[own], 1)
        for n in BARS:
            if n != own:
                eng.wait_ge(S[n], k)

    with nc.Block() as block:

        @block.vector
        def _(v):
            v.memset(Fv(COLS_O, PAR_O - COLS_O), 1.0)
            v.memset(Fv(THR_O + 16, 8), -1.0)
            v.memset(Fv(THR_O + 32, 8), 0.0)
            v.memset(Fv(ONESF_O, 128), 1.0)
            bar(v, "b_ve", 1)
            for p in range(2):
                if p == 0:
                    for k in range(16):
                        t, h = k // 2, k % 2
                        v.wait_ge(S["sm_x"], 16 * (k + 1))
                        col = C_S1A if h == 0 else C_S1B
                        ins = v.tensor_reduce(Fv(COL(col), 64)[:, t:t + 1],
                                              Fv(XST_O + (k % 2) * 1024, 1024), AX.X, ALU.add)
                        vinc(ins, ("s1", 0, k))
                if p == 0:
                    for k in range(32):
                        v.wait_ge(S["gm_ff"], 16 * (k + 1))
                        ins = v.tensor_reduce(Fv(FA_O, 64)[:, k:k + 1],
                                              Fv(FFWR_O + (k % 2) * 1024, 1024), AX.X, ALU.add,
                                              apply_absolute_value=True)
                        vinc(ins, ("fa", k))
                v.wait_ge(S["am_st"], 16 * p + 16)

                def cc_(i):
                    return Fv(COL(i), 64)

                v.tensor_tensor(cc_(C_S1S), cc_(C_S1A), cc_(C_S1B), ALU.add)
                v.tensor_tensor(cc_(C_S2S), cc_(C_S2A), cc_(C_S2B), ALU.add)
                v.tensor_scalar(cc_(C_MU), cc_(C_S1S), 1.0 / 2048.0, None, ALU.mult)
                v.tensor_scalar(cc_(C_E2), cc_(C_S2S), 1.0 / 2048.0, None, ALU.mult)
                v.tensor_tensor(cc_(C_VAR1), cc_(C_MU), cc_(C_MU), ALU.mult)
                v.tensor_tensor(cc_(C_VAR1), cc_(C_E2), cc_(C_VAR1), ALU.subtract)
                ins = v.tensor_scalar(cc_(C_A1), cc_(C_VAR1), EPS, None, ALU.add)
                vinc(ins, ("rm1", p))
                v.wait_ge(S["am_sq"], 2 * p + 1)
                v.reciprocal(cc_(C_RS), cc_(C_T1))
                v.reciprocal(cc_(C_RA), cc_(C_A1))
                v.tensor_tensor(cc_(C_VAR2), cc_(C_VAR1), cc_(C_RA), ALU.mult)
                ins = v.tensor_scalar(cc_(C_B2), cc_(C_VAR2), EPS, None, ALU.add)
                vinc(ins, ("rm2", p))
                v.wait_ge(S["am_sq"], 2 * p + 2)
                v.reciprocal(cc_(C_RS2), cc_(C_T2))
                v.tensor_tensor(cc_(C_ALPHA), cc_(C_RS), cc_(C_RS2), ALU.mult)
                v.scalar_tensor_tensor(cc_(C_BETA), cc_(C_MU), -1.0, cc_(C_ALPHA), ALU.mult, ALU.mult)
                v.tensor_scalar(cc_(C_C1), cc_(C_RS), 1.0, None, ALU.add)
                ins = v.scalar_tensor_tensor(cc_(C_C2), cc_(C_MU), -1.0, cc_(C_RS), ALU.mult, ALU.mult)
                vinc(ins, ("row", p))
                if p == 0:
                    v.memset(Fv(FA_O + 32, 32), 0.0)
                    v.memset(Fv(XST_O, 512), 0.0)
                    ins = v.tensor_reduce(Fv(XST_O, 512)[:, 0:1], Fv(FA_O, 64), AX.X, ALU.add)
                    vinc(ins, ("facol",))
                    v.wait_ge(S["pm_bc"], 1)
                    v.tensor_scalar(Fv(THR_O, 8), psB[:, 1024:1032], 0.5 / 4194304.0, 0.5e-5,
                                    ALU.mult, ALU.add)
                    v.memset(Fv(XST_O, 512), 0.0)
                    ins = v.tensor_scalar(Fv(THR_O + 8, 8), Fv(THR_O, 8), -1.0, None, ALU.mult)
                    vinc(ins, ("thr",))
                v.tensor_copy(Fv(ABTIN_O, 128)[:, 0:8], Fv(COL(C_ALPHA), 64)[:, 8 * p:8 * p + 8])
                ins = v.tensor_copy(Fv(ABTIN_O, 128)[:, 8:16], Fv(COL(C_BETA), 64)[:, 8 * p:8 * p + 8])
                vinc(ins, ("abpack", p))
                v.wait_ge(S["pm_bc"], 2 if p == 0 else 5)
                ins = v.tensor_copy(Fv(ABTEV_O, 128)[0:16, :], psB[0:16, 0:128])
                vinc(ins, ("abev", p))
                v.memset(Fv(AROW_O, 1024), 0.0)
                ins = v.memset(Fv(BROW_O, 1024), 0.0)
                vinc(ins, ("abz", p))
                for c in range(NCH):
                    v.wait_ge(S["gm_xt"], 16 * (16 * p + c + 1))
                    v.wait_ge(S["pm_bc"], 4 if p == 0 else 7)
                    xb_ = Fv(XTB_O + (c % 2) * 1024, 1024) if p == 0 else \
                        Fv([XTB_O, XTB_O + 1024, FFWR_O, FFWR_O + 1024][c % 4], 1024)
                    v.tensor_tensor(xb_, xb_, psB[:, 0:1024], ALU.mult)
                    ins = v.tensor_tensor(xb_, xb_, psB[:, 1024:2048], ALU.add)
                    vinc(ins, ("ut", p, c))
                bar(v, "b_ve", 2 + 2 * p)
                for j in range(NPAIR):
                    if j < 8:
                        for s_ in range(2):
                            step = 2 * j + s_
                            v.wait_ge(S["pm_rh"], F_rh(p, step + 1))
                            v.wait_ge(S["sm_tab"], 16 * (16 * p + 2 * j + 2))
                            src = B2(KBF_O if s_ == 0 else VBF_O, 1024)
                            dst = B2(KROT_O if s_ == 0 else VROT_O, 1024)
                            pss = psA[:, 0:1024] if s_ == 0 else psA[:, 1024:2048]
                            ins = v.tensor_tensor(B2(T1B_O, 1024), pss, tab_sin(j), ALU.mult)
                            vinc(ins, ("t1", p, step))
                            v.wait_ge(S["am_evac"], F_evac(p, step + 1))
                            v.tensor_tensor(dst, src, tab_cos(j), ALU.mult)
                            ins = v.tensor_tensor(dst, dst, B2(T1B_O, 1024), ALU.add)
                            vinc(ins, ("rc", p, step))
                    for s_, nko in ((0, NK2_O), (1, NV2_O)):
                        step = 2 * j + s_
                        v.wait_ge(S["am_ksq"], F_ksq(p, step + 1))
                        if j == 0:
                            ins = v.tensor_copy(B2(nko, 1024), B2(KSQ_O, 1024))
                        else:
                            ins = v.tensor_tensor(B2(nko, 1024), B2(KSQ_O, 1024), B2(nko, 1024), ALU.add)
                        vinc(ins, ("nk", p, step))
                    if j >= 2:
                        v.wait_ge(S["gm_pe"], 16 * (16 * p + j - 1))
                    ins = v.tensor_tensor(B2(PBUF_O + (j % 2) * 1024, 1024),
                                          B2(KROT_O, 1024), B2(VROT_O, 1024), ALU.mult)
                    vinc(ins, ("p", p, j))
                    if p == 0 and 3 <= j <= 10:
                        for h2 in range(2):
                            kb = 2 * (j - 3) + h2
                            v.wait_ge(S["sm_x"], 16 * (16 + kb + 1))
                            colb = C_S1A if kb % 2 == 0 else C_S1B
                            ins = v.tensor_reduce(
                                Fv(COL(colb), 64)[:, 8 + kb // 2:9 + kb // 2],
                                Fv(XST_O + (kb % 2) * 1024, 1024), AX.X, ALU.add)
                            vinc(ins, ("s1", 1, kb))
                    if p == 1 and j >= 8:
                        for h in range(2):
                            k = 2 * (j - 8) + h
                            v.wait_ge(S["gm_ff"], 16 * (32 + k + 1))
                            fw = Fv(FFWR_O + (k % 2) * 1024, 1024)
                            v.tensor_scalar(B2(M1B_O, 1024), fw, Fv(THR_O, 8)[:, 0:1], None, ALU.is_gt)
                            ins = v.scalar_tensor_tensor(
                                B2(WCAT_C(k // 2) + 1024 * (k % 2), 1024), fw,
                                Fv(THR_O + 8, 8)[:, 0:1], B2(M1B_O, 1024), ALU.is_ge, ALU.add)
                            vinc(ins, ("ffq", k))
                v.wait_ge(S["pm_nm"], 2 * p + 1)
                ins = v.tensor_copy(Fv(NRMA_O, 1024), psA[:, 0:1024])
                vinc(ins, ("nkc", p))
                v.wait_ge(S["pm_nm"], 2 * p + 2)
                ins = v.tensor_tensor(Fv(NRMB_O, 1024), Fv(NRMA_O, 1024), psA[:, 1024:2048], ALU.mult)
                vinc(ins, ("prod", p))
                v.wait_ge(S["am_sp"], p + 1)
                v.reciprocal(Fv(NRMB_O, 1024), Fv(NRMA_O, 1024))
                ins = v.tensor_copy(B2(WBB_O, 1024), Fv(NRMB_O, 1024))
                vinc(ins, ("wb", p))
                PRET4 = [PRET_O, PRET_O + 1024, KROT_O, VROT_O]
                FFW4 = [FFWR_O, FFWR_O + 1024, XST_O, XST_O + 1024]
                for j in range(NPAIR):
                    v.wait_ge(S["gm_pr"], 16 * (16 * p + j + 1))
                    ins = v.scalar_tensor_tensor(
                        B2(ADUMP_O, 1024), B2(PRET4[j % 4], 1024),
                        0.0, B2(WBB_O, 1024), ALU.bypass, ALU.mult,
                        accum_out=Fv(APART_O, 32)[:, 16 * p + j:16 * p + j + 1])
                    vinc(ins, ("ap", p, j))
                    if p == 1:
                        k = 16 + j
                        v.wait_ge(S["gm_ff"], 16 * (32 + k + 1))
                        fw = Fv(FFW4[k % 4], 1024)
                        v.tensor_scalar(B2(M1B_O, 1024), fw, Fv(THR_O, 8)[:, 0:1], None, ALU.is_gt)
                        ins = v.scalar_tensor_tensor(
                            B2(WCAT_C(k // 2) + 1024 * (k % 2), 1024), fw,
                            Fv(THR_O + 8, 8)[:, 0:1], B2(M1B_O, 1024), ALU.is_ge, ALU.add)
                        vinc(ins, ("ffq", k))
                if p == 1:
                    v.tensor_tensor(Fv(A2_O, 16), Fv(APART_O, 32)[:, 0:16],
                                    Fv(APART_O, 32)[:, 16:32], ALU.add)
                    v.memset(Fv(ABTEV_O, 64), 0.0)
                    v.memset(Fv(ABTEV_O, 64), 0.0)
                    v.wait_ge(S["sm_par"], 16 * 9)
                    for g_ in range(4):
                        ins = v.tensor_tensor(Fv(FA_O + 16 * g_, 16), Fv(A2_O, 16),
                                              Fv(BT16_O + 16 * g_, 16), ALU.mult)
                    vinc(ins, ("a2",))
                    v.wait_ge(S["gm_misc"], 16 * 8)
                    v.tensor_tensor(Fv(FA_O, 64), Fv(FA_O, 64), Fv(BT16_O, 64), ALU.mult)
                    v.memset(Fv(ABTIN_O, 64), 0.0)
                    v.memset(Fv(ABTEV_O, 64), 0.0)
                    v.tensor_tensor(Fv(ABTIN_O, 32), Fv(FA_O, 32), Fv(FA_O + 32, 32), ALU.add)
                    v.memset(Fv(A2_O, 16), 0.0)
                    v.memset(Fv(ABTEV_O, 64), 0.0)
                    v.tensor_tensor(Fv(A2_O, 16), Fv(ABTIN_O, 16), Fv(ABTIN_O + 16, 16), ALU.add)
                    v.memset(Fv(ABTEV_O, 64), 0.0)
                    v.memset(Fv(ABTEV_O, 64), 0.0)
                    ins = v.tensor_scalar(Fv(ACOL_O, 16), Fv(A2_O, 16), 0.5, 0.5, ALU.mult, ALU.add)
                    vinc(ins, ("af",))
                bar(v, "b_ve", 3 + 2 * p)
            ins = v.memset(Fv(BIAST_O, 2048), 0.0)
            vinc(ins, ("btz",))
            v.wait_ge(S["pm_bc"], 8)
            ins = v.tensor_copy(Fv(BIAST_O, 2048), psB[:, 0:2048])
            vinc(ins, ("bT",))
            for stt in range(NST):
                for h in range(2):
                    k = 2 * stt + h
                    v.wait_ge(S["sm_x2"], 16 * (k + 1))
                    if k >= 2:
                        v.wait_ge(S["sm_out"], 16 * (k - 1))
                    ob = Fv(OUTR_O + (k % 2) * 1024, 1024)
                    ins = v.scalar_tensor_tensor(
                        ob, Fv(X2R_O + (k % 2) * 1024, 1024),
                        Fv(COL(C_C1), 64)[:, stt:stt + 1],
                        Fv(BIAST_O, 2048)[:, 1024 * h:1024 * h + 1024], ALU.mult, ALU.add)
                    vinc(ins, ("w1", k))
                    v.wait_ge(S["pm_ph2"], stt + 1)
                    ins = v.scalar_tensor_tensor(
                        ob, ob, Fv(COL(C_C2), 64)[:, stt:stt + 1],
                        (psA if stt % 2 == 0 else psB)[:, 1024 * h:1024 * h + 1024],
                        ALU.add, ALU.add)
                    vinc(ins, ("o", k))
            bar(v, "b_ve", 6)

        @block.scalar
        def _(s):
            nAff = [0]
            nEv = [0]
            nKq = [0]
            bar(s, "b_ac", 1)
            s.wait_ge(S["sm_par"], 16 * 8)
            for p in range(2):
                if p == 0:
                    for k in range(16):
                        t, h = k // 2, k % 2
                        s.wait_ge(S["sm_x"], 16 * (k + 1))
                        col = C_S2A if h == 0 else C_S2B
                        s.activation(B2(DUMPS_O, 1024), Fv(XST_O + (k % 2) * 1024, 1024),
                                     AF.Square,
                                     accum_out=Fv(COL(col), 64)[:, t:t + 1]).then_inc(S["am_st"], 1)
                s.wait_ge(S["vm"], M[("rm1", p)])
                s.activation(Fv(COL(C_T1), 64), Fv(COL(C_A1), 64), AF.Sqrt).then_inc(S["am_sq"], 1)
                s.wait_ge(S["vm"], M[("rm2", p)])
                s.activation(Fv(COL(C_T2), 64), Fv(COL(C_B2), 64), AF.Sqrt).then_inc(S["am_sq"], 1)
                for c in range(NCH):
                    s.wait_ge(S["vm"], M[("ut", p, c)])
                    xb_ = Fv(XTB_O + (c % 2) * 1024, 1024) if p == 0 else \
                        Fv([XTB_O, XTB_O + 1024, FFWR_O, FFWR_O + 1024][c % 4], 1024)
                    s.activation(B3(XN2_O, 16, 1024)[:, c, :], xb_, AF.Identity,
                                 bias=Fv(BCOL_O, 16)[:, c:c + 1],
                                 scale=Fv(GCOL_O, 16)[:, c:c + 1]).then_inc(S["am_aff"], 1)
                    if c >= 2:
                        s.wait_ge(S["sm_fe"], 16 * 8 * (16 * p + c - 1))
                    s.activation(B2(FINB_O + (c % 2) * 1024, 1024), xb_, AF.Identity,
                                 bias=Fv(LNB_O, 16)[:, c:c + 1],
                                 scale=Fv(LNG_O, 16)[:, c:c + 1]).then_inc(S["am_aff"], 1)
                    nAff[0] += 2
                    assert nAff[0] == F_aff(p, 2 * c + 2)
                bar(s, "b_ac", 2 + 2 * p)
                for j in range(NPAIR):
                    for s_ in range(2):
                        step = 2 * j + s_
                        q = (16 if s_ == 0 else 32) + j
                        s.wait_ge(S["pm_mm"], F_mm(p, step + 1))
                        if j < 8:
                            dst = B2(KBF_O if s_ == 0 else VBF_O, 1024)
                            if j >= 1:
                                s.wait_ge(S["vm"], M[("rc", p, 2 * (j - 1) + s_)])
                        else:
                            dst = B2(KROT_O if s_ == 0 else VROT_O, 1024)
                            s.wait_ge(S["vm"], M[("p", p, j - 1)])
                        pss = psA[:, 0:1024] if s_ == 0 else psA[:, 1024:2048]
                        s.activation(dst, pss, AF.Identity,
                                     bias=Fv(BIN_O, 48)[:, q:q + 1]).then_inc(S["am_evac"], 1)
                        nEv[0] += 1
                        assert nEv[0] == F_evac(p, step + 1)
                    for s_ in range(2):
                        step = 2 * j + s_
                        src = B2(KROT_O if s_ == 0 else VROT_O, 1024)
                        if j < 8:
                            s.wait_ge(S["vm"], M[("rc", p, step)])
                        if s_ == 1:
                            s.wait_ge(S["vm"], M[("nk", p, 2 * j)])
                        s.activation(B2(KSQ_O, 1024), src, AF.Square).then_inc(S["am_ksq"], 1)
                        nKq[0] += 1
                        assert nKq[0] == F_ksq(p, step + 1)
                    if p == 0 and 3 <= j <= 10:
                        for h2 in range(2):
                            kb = 2 * (j - 3) + h2
                            s.wait_ge(S["sm_x"], 16 * (16 + kb + 1))
                            colb = C_S2A if kb % 2 == 0 else C_S2B
                            s.activation(B2(MB_O, 1024), Fv(XST_O + (kb % 2) * 1024, 1024),
                                         AF.Square,
                                         accum_out=Fv(COL(colb), 64)[:, 8 + kb // 2:9 + kb // 2]
                                         ).then_inc(S["am_st"], 1)
                    if p == 1 and j >= 8:
                        for h in range(2):
                            k = 2 * (j - 8) + h
                            s.wait_ge(S["vm"], M[("ffq", k)])
                            wslot = B2(WCAT_C(k // 2) + 1024 * (k % 2), 1024)
                            s.activation(wslot, wslot, AF.Identity,
                                         bias=Fv(THR_O + 16, 8)[:, 0:1]).then_inc(S["am_ffq"], 1)
                s.wait_ge(S["vm"], M[("prod", p)])
                s.activation(Fv(NRMA_O, 1024), Fv(NRMB_O, 1024), AF.Sqrt).then_inc(S["am_sp"], 1)
                if p == 1:
                    pass
                for k in range(NQ):
                    step = 32 + k
                    s.wait_ge(S["pm_mm"], F_mm(p, step + 1))
                    if k >= 2:
                        s.wait_ge(S["sm_qe"], 16 * 8 * (16 * p + k - 1))
                    pss = psA[:, 0:1024] if step % 2 == 0 else psA[:, 1024:2048]
                    s.activation(B2(QBF_O + (k % 2) * 1024, 1024), pss, AF.Identity,
                                 bias=Fv(BIN_O, 48)[:, k:k + 1]).then_inc(S["am_evac"], 1)
                    nEv[0] += 1
                if p == 1:
                    for kf in range(16, 32):
                        s.wait_ge(S["vm"], M[("ffq", kf)])
                        wslot = B2(WCAT_C(kf // 2) + 1024 * (kf % 2), 1024)
                        s.activation(wslot, wslot, AF.Identity,
                                     bias=Fv(THR_O + 16, 8)[:, 0:1]).then_inc(S["am_ffq"], 1)
                if p == 1:
                    s.wait_ge(S["vm"], M[("af",)])
                    for idx, j in enumerate(range(10, 16)):
                        s.wait_ge(S["gm_wc"], 16 * (idx + 1))
                        s.activation(B2(WCAT_C(16 + j), 2048), B2(WCAT_C(16 + j), 2048),
                                     AF.Identity,
                                     scale=Fv(ACOL_O, 16)[:, j:j + 1]).then_inc(S["am_ffq"], 1)
                bar(s, "b_ac", 3 + 2 * p)
            for stt in range(NST):
                for h in range(2):
                    k = 2 * stt + h
                    if k >= 2:
                        s.wait_ge(S["vm"], M[("w1", k - 2)])
                    s.dma_start(out=Fv(X2R_O + (k % 2) * 1024, 1024),
                                in_=x_d[128 * stt:128 * (stt + 1), 1024 * h:1024 * h + 1024]
                                ).then_inc(S["sm_x2"], 16)
                if stt == 0:
                    for idx, j in enumerate(range(10)):
                        s.wait_ge(S["gm_wc"], 16 * (7 + idx))
                        s.activation(B2(WCAT_C(16 + j), 2048), B2(WCAT_C(16 + j), 2048),
                                     AF.Identity,
                                     scale=Fv(ACOL_O, 16)[:, j:j + 1]).then_inc(S["am_ffq"], 1)
                if stt >= 1:
                    for h in range(2):
                        k = 2 * (stt - 1) + h
                        s.wait_ge(S["vm"], M[("o", k)])
                        s.dma_start(out=out_d[128 * (stt - 1):128 * stt, 1024 * h:1024 * h + 1024],
                                    in_=Fv(OUTR_O + (k % 2) * 1024, 1024)).then_inc(S["sm_out"], 16)
            for h in range(2):
                k = 30 + h
                s.wait_ge(S["vm"], M[("o", k)])
                s.dma_start(out=out_d[128 * 15:2048, 1024 * h:1024 * h + 1024],
                            in_=Fv(OUTR_O + (k % 2) * 1024, 1024)).then_inc(S["sm_out"], 16)
            bar(s, "b_ac", 6)

        @block.tensor
        def _(te):
            pmm = [0]
            pbc = [0]

            def bcinc(ins):
                ins.then_inc(S["pm_bc"], 1)
                pbc[0] += 1
                return pbc[0]

            bar(te, "b_pe", 1)
            te.wait_ge(S["sm_par"], 16 * 8)
            for p in range(2):
                if p == 0:
                    te.wait_ge(S["vm"], M[("facol",)])
                    assert bcinc(te.matmul(psB[:, 1024:1536], Fv(ONESF_O, 128),
                                           Fv(XST_O, 512),
                                           start=True, stop=True)) == 1
                te.wait_ge(S["vm"], M[("abpack", p)])
                assert bcinc(te.transpose(psB[:, 0:128], Fv(ABTIN_O, 128),
                                          Fv(IDENT_O, 128))) == (2 if p == 0 else 5)
                te.wait_ge(S["gm_misc"], 16 * (3 + 3 * p))
                for half in range(2):
                    ins = te.matmul(psB[:, 512 * half:512 * half + 512], Fv(ONESF_O, 128),
                                    Fv(AROW_O, 1024)[:, 512 * half:512 * half + 512],
                                    start=True, stop=True)
                assert bcinc(ins) == (3 if p == 0 else 6)
                for half in range(2):
                    ins = te.matmul(psB[:, 1024 + 512 * half:1024 + 512 * half + 512],
                                    Fv(ONESF_O, 128),
                                    Fv(BROW_O, 1024)[:, 512 * half:512 * half + 512],
                                    start=True, stop=True)
                assert bcinc(ins) == (4 if p == 0 else 7)
                bar(te, "b_pe", 2 + 2 * p)
                for i in range(48):
                    te.wait_ge(S["gm_w"], 16 * (48 * p + i + 1))
                    if i >= 2:
                        prev = i - 2
                        if prev < 16:
                            te.wait_ge(S["vm"], M[("t1", p, prev)])
                        elif i == 32:
                            te.wait_ge(S["vm"], M[("nkc", p)])
                        elif i == 33:
                            te.wait_ge(S["vm"], M[("prod", p)])
                        else:
                            te.wait_ge(S["am_evac"], F_evac(p, prev + 1))
                    pss = psA[:, 0:1024] if i % 2 == 0 else psA[:, 1024:2048]
                    wsl = B3(WSLAB_O + (i % 2) * 2048, 16, 128)
                    xn2 = B3(XN2_O, 16, 1024)
                    for c in range(NCH):
                        for g_ in range(2):
                            ins = te.matmul(pss[:, 512 * g_:512 * g_ + 512], wsl[:, c, :],
                                            xn2[:, c, 512 * g_:512 * g_ + 512],
                                            start=(c == 0), stop=(c == NCH - 1))
                    ins.then_inc(S["pm_mm"], 1)
                    pmm[0] += 1
                    assert pmm[0] == F_mm(p, i + 1)
                    if i < 32 and i % 2 == 1 and i // 2 < 8:
                        j = i // 2
                        for s_ in range(2):
                            step = 2 * j + s_
                            te.wait_ge(S["am_evac"], F_evac(p, step + 1))
                            src = B2(KBF_O if s_ == 0 else VBF_O, 1024)
                            pss2 = psA[:, 0:1024] if s_ == 0 else psA[:, 1024:2048]
                            for g_ in range(2):
                                ins = te.matmul(pss2[:, 512 * g_:512 * g_ + 512], B2(PSW_O, 128),
                                                src[:, 512 * g_:512 * g_ + 512],
                                                start=True, stop=True)
                            ins.then_inc(S["pm_rh"], 1)
                    if i == 31:
                        te.wait_ge(S["vm"], M[("nk", p, 31)])
                        te.wait_ge(S["am_evac"], F_evac(p, 31))
                        for g_ in range(2):
                            ins = te.matmul(psA[:, 512 * g_:512 * g_ + 512], B2(ONESB_O, 128),
                                            B2(NK2_O, 1024)[:, 512 * g_:512 * g_ + 512],
                                            start=True, stop=True)
                        ins.then_inc(S["pm_nm"], 1)
                        te.wait_ge(S["am_evac"], F_evac(p, 32))
                        for g_ in range(2):
                            ins = te.matmul(psA[:, 1024 + 512 * g_:1024 + 512 * g_ + 512],
                                            B2(ONESB_O, 128),
                                            B2(NV2_O, 1024)[:, 512 * g_:512 * g_ + 512],
                                            start=True, stop=True)
                        ins.then_inc(S["pm_nm"], 1)
                bar(te, "b_pe", 3 + 2 * p)
            te.wait_ge(S["sm_par"], 16 * 11)
            for g_ in range(4):
                ins = te.matmul(psB[:, 512 * g_:512 * g_ + 512], Fv(ONESF_O, 128),
                                Fv(BIAST_O, 2048)[:, 512 * g_:512 * g_ + 512],
                                start=True, stop=True)
            assert bcinc(ins) == 8
            d_seq = list(range(16)) + list(range(26, 32)) + list(range(16, 26))
            wc_idx = {26 + j: j + 1 for j in range(6)}
            wc_idx.update({16 + j: 7 + j for j in range(10)})
            for stt in range(NST):
                te.wait_ge(S["sm_sr"], 16 * (2 * stt + 2))
                if stt >= 2:
                    te.wait_ge(S["vm"], M[("o", 2 * (stt - 2) + 1)])
                if stt < 2:
                    te.wait_ge(S["vm"], M[("bT",)])
                pss = psA if stt % 2 == 0 else psB
                fsl = B3(SLABF_O + (stt % 4) * 2048, 16, 128)
                ysl = B3(SLABQ_O + (stt % 4) * 2048, 16, 128)
                if stt == 1:
                    te.wait_ge(S["am_ffq"], 48)
                for di, d in enumerate(d_seq):
                    if d >= 16 and stt == 0:
                        te.wait_ge(S["am_ffq"], 32 + wc_idx[d])
                    statn = fsl[:, d, :] if d < 16 else ysl[:, d - 16, :]
                    for g_ in range(4):
                        ins = te.matmul(pss[:, 512 * g_:512 * g_ + 512], statn,
                                        B2(WCAT_C(d), 2048)[:, 512 * g_:512 * g_ + 512],
                                        start=(di == 0), stop=(di == 31))
                ins.then_inc(S["pm_ph2"], 1)
            bar(te, "b_pe", 6)

        @block.gpsimd
        def _(g):
            gmisc = [0]

            def gmi(ins):
                ins.then_inc(S["gm_misc"], 16)
                gmisc[0] += 1
                return gmisc[0]

            bar(g, "b_gp", 1)
            for p in range(2):
                if p == 0:
                    for k in range(32):
                        if k >= 2:
                            g.wait_ge(S["vm"], M[("fa", k - 2)])
                        g.dma_start(out=Fv(FFWR_O + (k % 2) * 1024, 1024),
                                    in_=ffw_d[k // 2][:, 1024 * (k % 2):1024 * (k % 2) + 1024]
                                    ).then_inc(S["gm_ff"], 16)
                g.wait_ge(S["vm"], M[("abev", p)])
                g.wait_ge(S["vm"], M[("abz", p)])
                assert gmi(g.dma_start(out=abd, in_=Fv(ABTEV_O, 128)[0:16, :])) == 1 + 3 * p
                g.wait_ge(S["gm_misc"], 16 * (1 + 3 * p))
                assert gmi(g.dma_start(out=Fv(AROW_O, 1024)[0:1, :],
                                       in_=abd[0:8].rearrange("c p -> (c p)")[None, :])) == 2 + 3 * p
                assert gmi(g.dma_start(out=Fv(BROW_O, 1024)[0:1, :],
                                       in_=abd[8:16].rearrange("c p -> (c p)")[None, :])) == 3 + 3 * p
                for c in range(NCH):
                    if p == 0:
                        if c >= 2:
                            g.wait_ge(S["am_aff"], F_aff(p, 2 * (c - 1)))
                        g.dma_start(out=Fv(XTB_O + (c % 2) * 1024, 1024),
                                    in_=xt_d[c][:, 0:TH]).then_inc(S["gm_xt"], 16)
                    elif c >= 4:
                        g.wait_ge(S["am_aff"], F_aff(1, 2 * (c - 4) + 2))
                        g.dma_start(out=Fv([XTB_O, XTB_O + 1024, FFWR_O,
                                            FFWR_O + 1024][c % 4], 1024),
                                    in_=xt_d[c][:, TH:2 * TH]).then_inc(S["gm_xt"], 16)
                for i0 in range(2):
                    g.dma_start(out=B3(WSLAB_O + (i0 % 2) * 2048, 16, 128),
                                in_=win_d[(16, 32)[i0]]).then_inc(S["gm_w"], 16)
                bar(g, "b_gp", 2 + 2 * p)
                fo_list = []
                for j in range(NPAIR):
                    fo_list += [16 + j, 32 + j]
                fo_list += list(range(16))

                def wload(i):
                    if i >= 2:
                        g.wait_ge(S["pm_mm"], F_mm(p, i - 1))
                    g.dma_start(out=B3(WSLAB_O + (i % 2) * 2048, 16, 128),
                                in_=win_d[fo_list[i]]).then_inc(S["gm_w"], 16)

                for i in range(48):
                    if i + 2 < 48:
                        wload(i + 2)
                    if i < 32 and i % 2 == 1:
                        j = i // 2
                        if p == 1 and 7 <= j <= 14:
                            for h in range(2):
                                k = 2 * (j - 7) + h
                                if k >= 2:
                                    g.wait_ge(S["vm"], M[("ffq", k - 2)])
                                g.dma_start(out=Fv(FFWR_O + (k % 2) * 1024, 1024),
                                            in_=ffw_d[k // 2][:, 1024 * (k % 2):1024 * (k % 2) + 1024]
                                            ).then_inc(S["gm_ff"], 16)
                        g.wait_ge(S["vm"], M[("p", p, j)])
                        g.dma_start(out=ph[p][j], in_=B2(PBUF_O + (j % 2) * 1024, 1024)
                                    ).then_inc(S["gm_pe"], 16)
                    if i >= 32:
                        jr = i - 32
                        PRET4 = [PRET_O, PRET_O + 1024, KROT_O, VROT_O]
                        FFW4 = [FFWR_O, FFWR_O + 1024, XST_O, XST_O + 1024]
                        if p == 1:
                            k = 16 + jr
                            if k - 4 >= 16:
                                g.wait_ge(S["vm"], M[("ffq", k - 4)])
                            else:
                                g.wait_ge(S["vm"], M[("ffq", k - 2)])
                            if FFW4[k % 4] in (XST_O, XST_O + 1024) and k < 20:
                                g.wait_ge(S["vm"], M[("wb", p)])
                            g.dma_start(out=Fv(FFW4[k % 4], 1024),
                                        in_=ffw_d[k // 2][:, 1024 * (k % 2):1024 * (k % 2) + 1024]
                                        ).then_inc(S["gm_ff"], 16)
                        g.wait_ge(S["gm_pe"], 16 * (16 * p + jr + 1))
                        if jr >= 4:
                            g.wait_ge(S["vm"], M[("ap", p, jr - 4)])
                        if jr in (2, 3):
                            g.wait_ge(S["vm"], M[("p", p, 15)])
                        g.dma_start(out=B2(PRET4[jr % 4], 1024), in_=ph[p][jr]
                                    ).then_inc(S["gm_pr"], 16)
                if p == 0:
                    for j in range(10, 16):
                        g.dma_start(out=B2(WCAT_C(16 + j), 2048), in_=wout_d[j]
                                    ).then_inc(S["gm_wc"], 16)
                    for c in range(2):
                        g.dma_start(out=Fv(XTB_O + c * 1024, 1024),
                                    in_=xt_d[c][:, TH:2 * TH]).then_inc(S["gm_xt"], 16)
                    g.wait_ge(S["vm"], M[("fa", 31)])
                    for c in range(2, 4):
                        g.dma_start(out=Fv(FFWR_O + (c - 2) * 1024, 1024),
                                    in_=xt_d[c][:, TH:2 * TH]).then_inc(S["gm_xt"], 16)
                if p == 1:
                    g.wait_ge(S["vm"], M[("a2",)])
                    assert gmi(g.dma_start(out=ar_in, in_=Fv(FA_O, 64))) == 7
                    g.wait_ge(S["gm_misc"], 16 * 7)
                    g.collective_compute(
                        "AllReduce", ALU.add,
                        replica_groups=[[0, 1, 2, 3, 4, 5, 6, 7]],
                        ins=[ar_in[:, :]], outs=[ar_out[:, :]],
                    ).then_inc(S["cc"], 1)
                    g.wait_ge(S["cc"], 1)
                    assert gmi(g.dma_start(out=Fv(FA_O, 64), in_=ar_out)) == 8
                bar(g, "b_gp", 3 + 2 * p)
            g.wait_ge(S["vm"], M[("ap", 1, 15)])
            g.wait_ge(S["vm"], M[("ffq", 31)])
            g.wait_ge(S["gm_pe"], 16 * 32)
            for d in range(10):
                g.dma_start(out=B2(WCAT_C(16 + d), 2048), in_=wout_d[d]
                            ).then_inc(S["gm_wc"], 16)
            bar(g, "b_gp", 6)

        @block.sync
        def _(sy):
            for dst, src in ((Fv(GCOL_O, 16), gcol_d), (Fv(BCOL_O, 16), bcol_d),
                             (Fv(LNG_O, 16), lng_d), (Fv(LNB_O, 16), lnb_d),
                             (Fv(BIN_O, 48), bin_d), (B2(PSW_O, 128), psw_d),
                             (B2(ONESB_O, 128), onesb_d), (Fv(IDENT_O, 128), ident_d),
                             (Fv(ABTEV_O + 128, 64) if False else None, None)):
                if dst is None:
                    continue
                sy.dma_start(out=dst, in_=src).then_inc(S["sm_par"], 16)
            sy.dma_start(out=Fv(BT16_O, 64), in_=armask_d).then_inc(S["sm_par"], 16)
            bar(sy, "b_sy", 1)
            def xstload(p_, k):
                gt, h = 8 * p_ + k // 2, k % 2
                sy.dma_start(out=Fv(XST_O + (k % 2) * 1024, 1024),
                             in_=x_d[128 * gt:128 * (gt + 1), 1024 * h:1024 * h + 1024]
                             ).then_inc(S["sm_x"], 16)

            for p in range(2):
                if p == 0:
                    for k in range(16):
                        if k >= 2:
                            sy.wait_ge(S["vm"], M[("s1", 0, k - 2)])
                            sy.wait_ge(S["am_st"], k - 1)
                        xstload(0, k)
                for c in range(NCH):
                    sy.wait_ge(S["am_aff"], F_aff(p, 2 * c + 2))
                    for t in range(8):
                        sy.dma_start(out=fh[p][t, :, c, :],
                                     in_=B2(FINB_O + (c % 2) * 1024, 1024)[:, 128 * t:128 * (t + 1)]
                                     ).then_inc(S["sm_fe"], 16)
                bar(sy, "b_sy", 2 + 2 * p)
                for j in range(8):
                    if j >= 2:
                        sy.wait_ge(S["vm"], M[("rc", p, 2 * (j - 2) + 1)])
                    sy.dma_start(out=tab_cos(j), in_=cost_d[p][j]).then_inc(S["sm_tab"], 16)
                    sy.dma_start(out=tab_sin(j), in_=sint_d[p][j]).then_inc(S["sm_tab"], 16)
                    if p == 0 and j == 1:
                        xstload(1, 0)
                        xstload(1, 1)
                    if p == 0 and j >= 5:
                        for kb in (2 * (j - 4), 2 * (j - 4) + 1):
                            sy.wait_ge(S["vm"], M[("s1", 1, kb - 2)])
                            sy.wait_ge(S["am_st"], 16 + kb - 1)
                            xstload(1, kb)
                if p == 0:
                    for kb in range(8, 16):
                        sy.wait_ge(S["vm"], M[("s1", 1, kb - 2)])
                        sy.wait_ge(S["am_st"], 16 + kb - 1)
                        xstload(1, kb)
                for k in range(NQ):
                    sy.wait_ge(S["am_evac"], F_evac(p, 32 + k + 1))
                    for t in range(8):
                        sy.dma_start(out=qh[p][t, :, k, :],
                                     in_=B2(QBF_O + (k % 2) * 1024, 1024)[:, 128 * t:128 * (t + 1)]
                                     ).then_inc(S["sm_qe"], 16)
                bar(sy, "b_sy", 3 + 2 * p)
            sy.wait_ge(S["vm"], M[("btz",)])
            sy.dma_start(out=Fv(BIAST_O, 2048)[0:1, :], in_=boutr_d).then_inc(S["sm_par"], 16)
            sy.dma_start(out=Fv(BIAST_O, 2048)[1:2, :], in_=fbr_d).then_inc(S["sm_par"], 16)
            sy.wait_ge(S["sm_fe"], 16 * 8 * 32)
            sy.wait_ge(S["sm_qe"], 16 * 8 * 32)
            for stt in range(NST):
                p, stl = stt // 8, stt % 8
                if stt >= 4:
                    sy.wait_ge(S["pm_ph2"], stt - 3)
                sy.dma_start(out=B3(SLABF_O + (stt % 4) * 2048, 16, 128),
                             in_=fh[p][stl]).then_inc(S["sm_sr"], 16)
                sy.dma_start(out=B3(SLABQ_O + (stt % 4) * 2048, 16, 128),
                             in_=qh[p][stl]).then_inc(S["sm_sr"], 16)
            bar(sy, "b_sy", 6)
            sy.wait_ge(S["sm_out"], 16 * 32)

    es.close()
    return nc


_NC_CACHE = [None]


def kernel(L0_attn_X, L0_attn_W_in, L0_attn_b_in, L0_attn_W_out, L0_attn_b_out,
           L0_attn_gamma, L0_attn_beta, L0_ffn_weight, L0_ffn_bias,
           L0_ffn_ln_gamma, L0_ffn_ln_beta):
    X = np.ascontiguousarray(np.asarray(L0_attn_X, dtype=np.float32)).reshape(16384, 2048)
    W_in = np.asarray(L0_attn_W_in, dtype=np.float32)
    W_out = np.asarray(L0_attn_W_out, dtype=np.float32)
    ffw = np.asarray(L0_ffn_weight, dtype=np.float32)

    win_r = np.ascontiguousarray(
        W_in.reshape(16, 128, 48, 128).transpose(2, 1, 0, 3))
    wout_r = np.ascontiguousarray(W_out.reshape(16, 128, 2048))
    ffwT = np.ascontiguousarray(ffw.T).reshape(16, 128, 2048)

    def colform(vv, w):
        return np.ascontiguousarray(np.asarray(vv, dtype=np.float32).reshape(w, 128).T)

    gcol = colform(L0_attn_gamma, 16)
    bcol = colform(L0_attn_beta, 16)
    lngcol = colform(L0_ffn_ln_gamma, 16)
    lnbcol = colform(L0_ffn_ln_beta, 16)
    bincol = colform(L0_attn_b_in, 48)
    boutrow = np.ascontiguousarray(np.asarray(L0_attn_b_out, dtype=np.float32).reshape(1, 2048))
    fbrow = np.ascontiguousarray(np.asarray(L0_ffn_bias, dtype=np.float32).reshape(1, 2048))

    rot = 1024
    inv = (np.float32(1.0) / (np.float32(10000.0) **
                              (np.arange(0, rot, 2, dtype=np.float32) / np.float32(rot)))
           ).astype(np.float32)
    pos = np.arange(4096, dtype=np.float32)
    fr = np.repeat((pos[:, None] * inv[None, :]).astype(np.float32), 2, axis=1)
    cosf = np.cos(fr).astype(np.float32).T
    sinf = np.sin(fr).astype(np.float32).T

    psw = np.zeros((128, 128), dtype=np.float32)
    for i in range(64):
        psw[2 * i + 1, 2 * i] = -1.0
        psw[2 * i, 2 * i + 1] = 1.0
    psw_bf = psw.astype(BFH)
    onesb = np.ones((128, 128), dtype=BFH)
    armasks = []
    for c in range(NCORE):
        m_ = np.zeros((128, 64), dtype=np.float32)
        m_[:, 16 * (c // 2):16 * (c // 2) + 16] = 1.0
        armasks.append(m_)
    ident = np.eye(128, dtype=np.float32)

    if _NC_CACHE[0] is None:
        _NC_CACHE[0] = build_nc()
    nc = _NC_CACHE[0]

    in_maps = []
    for c in range(NCORE):
        b, half = c // 2, c % 2
        r0 = b * 4096 + half * 2048
        xc = np.ascontiguousarray(X[r0:r0 + 2048])
        xtc = np.ascontiguousarray(xc.T).reshape(16, 128, 2048)
        s0 = half * 2048
        ctp = np.ascontiguousarray(
            cosf[:, s0:s0 + 2048].reshape(8, 128, 2, 1024).transpose(2, 0, 1, 3)).astype(BFH)
        stp = np.ascontiguousarray(
            sinf[:, s0:s0 + 2048].reshape(8, 128, 2, 1024).transpose(2, 0, 1, 3)).astype(BFH)
        in_maps.append({
            "x": xc, "xt": xtc, "win": win_r, "wout": wout_r, "ffw": ffwT,
            "cost": ctp, "sint": stp,
            "gcol": gcol, "bcol": bcol, "lngcol": lngcol, "lnbcol": lnbcol,
            "bincol": bincol, "boutrow": boutrow, "fbrow": fbrow,
            "pairswap": psw_bf, "onesb": onesb, "ident": ident, "armask": armasks[c],
        })

    res = run_bass_kernel_spmd(nc, in_maps, core_ids=list(range(NCORE)))
    out = np.empty((16384, 2048), dtype=np.float32)
    for c in range(NCORE):
        b, half = c // 2, c % 2
        r0 = b * 4096 + half * 2048
        out[r0:r0 + 2048] = res.results[c]["out"]
    return out.reshape(4, 4096, 2048)


# version 24
# speedup vs baseline: 1.0038x; 1.0038x over previous
import numpy as np
import ml_dtypes
import contextlib
import concourse.bass as bass
import concourse.mybir as mybir
from concourse.bass_utils import run_bass_kernel_spmd

F32 = mybir.dt.float32
BF16 = mybir.dt.bfloat16
AF = mybir.ActivationFunctionType
ALU = mybir.AluOpType
AX = mybir.AxisListType
BFH = ml_dtypes.bfloat16

TH = 1024
NCH = 16
NPAIR = 16
NQ = 16
NST = 16
EPS = 1e-5
NCORE = 8

XN2_O = 0
SLABF_O = 0
SLABQ_O = 8192
WSLAB_O = 16384
QBF_O = 20480
DUMPS_O = 20480
WCAT_O = 22528


def WCAT_C(d):
    return WCAT_O + 2048 * d


TABC_O = 55296
FINB_O = 55296
KBF_O = 59392
VBF_O = 60416
KROT_O = 61440
VROT_O = 62464
KSQ_O = 63488
T1B_O = 64512
NK2_O = 65536
NV2_O = 66560
PBUF_O = 67584
PRET_O = 69632
WBB_O = 71680
M1B_O = 72704
MB_O = 73728
ADUMP_O = KBF_O
PSW_O = 74752
ONESB_O = 74880
BF_TOT = 88064

XST_O = 0
NRMA_O = 0
NRMB_O = 1024
AROW_O = 0
BROW_O = 1024
X2R_O = 0
XTB_O = 2048
OUTR_O = 2048
FFWR_O = 4096
BIAST_O = 4096
COLS_O = 6144


def COL(i):
    return COLS_O + 64 * i


(C_S1A, C_S1B, C_S2A, C_S2B, C_MU, C_E2, C_VAR1, C_A1, C_RA, C_RS, C_VAR2,
 C_B2, C_RS2, C_ALPHA, C_BETA, C_C1, C_C2, C_T1, C_T2, C_S1S, C_S2S) = range(21)
MISC_O = COLS_O + 21 * 64
ABTIN_O = MISC_O
ABTEV_O = MISC_O + 128
BT16_O = MISC_O + 256
THR_O = MISC_O + 384
APART_O = MISC_O + 448
A2_O = MISC_O + 480
ACOL_O = MISC_O + 496
FA_O = MISC_O + 512
ONESF_O = MISC_O + 576
PAR_O = MISC_O + 704
GCOL_O = PAR_O
BCOL_O = PAR_O + 16
LNG_O = PAR_O + 32
LNB_O = PAR_O + 48
BIN_O = PAR_O + 64
BOUTC_O = PAR_O + 112
FBC_O = PAR_O + 128
ONES1_O = PAR_O + 144
IDENT_O = PAR_O + 272
F32_TOT = PAR_O + 400

BARS = ("b_sy", "b_gp", "b_ac", "b_ve", "b_pe")


def build_nc():
    nc = bass.Bass("TRN2", target_bir_lowering=False, detect_race_conditions=False)

    x_d = nc.dram_tensor("x", [2048, 2048], F32, kind="ExternalInput").ap()
    xt_d = nc.dram_tensor("xt", [NCH, 128, 2048], F32, kind="ExternalInput").ap()
    win_d = nc.dram_tensor("win", [48, 128, NCH, 128], F32, kind="ExternalInput").ap()
    wout_d = nc.dram_tensor("wout", [16, 128, 2048], F32, kind="ExternalInput").ap()
    ffw_d = nc.dram_tensor("ffw", [NCH, 128, 2048], F32, kind="ExternalInput").ap()
    cost_d = nc.dram_tensor("cost", [2, 8, 128, TH], BF16, kind="ExternalInput").ap()
    sint_d = nc.dram_tensor("sint", [2, 8, 128, TH], BF16, kind="ExternalInput").ap()
    gcol_d = nc.dram_tensor("gcol", [128, 16], F32, kind="ExternalInput").ap()
    bcol_d = nc.dram_tensor("bcol", [128, 16], F32, kind="ExternalInput").ap()
    lng_d = nc.dram_tensor("lngcol", [128, 16], F32, kind="ExternalInput").ap()
    lnb_d = nc.dram_tensor("lnbcol", [128, 16], F32, kind="ExternalInput").ap()
    bin_d = nc.dram_tensor("bincol", [128, 48], F32, kind="ExternalInput").ap()
    boutr_d = nc.dram_tensor("boutrow", [1, 2048], F32, kind="ExternalInput").ap()
    fbr_d = nc.dram_tensor("fbrow", [1, 2048], F32, kind="ExternalInput").ap()
    psw_d = nc.dram_tensor("pairswap", [128, 128], BF16, kind="ExternalInput").ap()
    armask_d = nc.dram_tensor("armask", [128, 64], F32, kind="ExternalInput").ap()
    onesb_d = nc.dram_tensor("onesb", [128, 128], BF16, kind="ExternalInput").ap()
    ident_d = nc.dram_tensor("ident", [128, 128], F32, kind="ExternalInput").ap()
    out_d = nc.dram_tensor("out", [2048, 2048], F32, kind="ExternalOutput").ap()

    qh = [nc.dram_tensor(f"qh{p}", [8, 128, NCH, 128], BF16).ap() for p in range(2)]
    fh = [nc.dram_tensor(f"fh{p}", [8, 128, NCH, 128], BF16).ap() for p in range(2)]
    ph = [nc.dram_tensor(f"ph{p}", [NPAIR, 128, TH], BF16).ap() for p in range(2)]
    abd = nc.dram_tensor("abd", [16, 128], F32).ap()
    ar_in = nc.dram_tensor("ar_in", [128, 64], F32).ap()
    ar_out = nc.dram_tensor("ar_out", [128, 64], F32, addr_space="Shared").ap()

    es = contextlib.ExitStack()
    hb = es.enter_context(nc.sbuf_tensor("hb", [128, BF_TOT], BF16))
    hf = es.enter_context(nc.sbuf_tensor("hf", [128, F32_TOT], F32))
    psA = es.enter_context(nc.psum_tensor("psA", [128, 2048], F32))
    psB = es.enter_context(nc.psum_tensor("psB", [128, 2048], F32))

    SEMS = list(BARS) + [
        "sm_par", "sm_x", "sm_tab", "sm_fe", "sm_qe", "sm_x2", "sm_sr", "sm_out",
        "gm_xt", "gm_ff", "gm_w", "gm_pe", "gm_pr", "gm_wc", "gm_misc", "cc",
        "am_st", "am_sq", "am_aff", "am_ffq", "am_evac", "am_ksq", "am_sp",
        "pm_mm", "pm_rh", "pm_nm", "pm_bc", "pm_ph2", "vm"]
    S = {n: es.enter_context(nc.semaphore(n)) for n in SEMS}

    M = {}
    vmc = [0]

    def vinc(ins, key=None):
        ins.then_inc(S["vm"], 1)
        vmc[0] += 1
        if key is not None:
            M[key] = vmc[0]
        return vmc[0]

    def B2(off, n):
        return hb[:, off:off + n]

    def B3(off, c, n):
        return hb[:, off:off + c * n].rearrange("p (c n) -> p c n", n=n)

    def Fv(off, n):
        return hf[:, off:off + n]

    def F_mm(p, k):
        return 48 * p + k

    def F_rh(p, k):
        return 16 * p + k

    def F_evac(p, k):
        return 48 * p + k

    def F_ksq(p, k):
        return 32 * p + k

    def F_aff(p, k):
        return 32 * p + k

    def tab_cos(j):
        return B2(TABC_O + (j % 2) * 2048, 1024)

    def tab_sin(j):
        return B2(TABC_O + (j % 2) * 2048 + 1024, 1024)

    def bar(eng, own, k):
        eng.sem_inc(S[own], 1)
        for n in BARS:
            if n != own:
                eng.wait_ge(S[n], k)

    with nc.Block() as block:

        @block.vector
        def _(v):
            v.memset(Fv(COLS_O, PAR_O - COLS_O), 1.0)
            v.memset(Fv(THR_O + 16, 8), -1.0)
            v.memset(Fv(THR_O + 32, 8), 0.0)
            v.memset(Fv(ONESF_O, 128), 1.0)
            bar(v, "b_ve", 1)
            for p in range(2):
                if p == 0:
                    for k in range(16):
                        t, h = k // 2, k % 2
                        v.wait_ge(S["sm_x"], 16 * (k + 1))
                        col = C_S1A if h == 0 else C_S1B
                        ins = v.tensor_reduce(Fv(COL(col), 64)[:, t:t + 1],
                                              Fv(XST_O + (k % 2) * 1024, 1024), AX.X, ALU.add)
                        vinc(ins, ("s1", 0, k))
                if p == 0:
                    for k in range(32):
                        v.wait_ge(S["gm_ff"], 16 * (k + 1))
                        ins = v.tensor_reduce(Fv(FA_O, 64)[:, k:k + 1],
                                              Fv(FFWR_O + (k % 2) * 1024, 1024), AX.X, ALU.add,
                                              apply_absolute_value=True)
                        vinc(ins, ("fa", k))
                v.wait_ge(S["am_st"], 16 * p + 16)

                def cc_(i):
                    return Fv(COL(i), 64)

                v.tensor_tensor(cc_(C_S1S), cc_(C_S1A), cc_(C_S1B), ALU.add)
                v.tensor_tensor(cc_(C_S2S), cc_(C_S2A), cc_(C_S2B), ALU.add)
                v.tensor_scalar(cc_(C_MU), cc_(C_S1S), 1.0 / 2048.0, None, ALU.mult)
                v.tensor_scalar(cc_(C_E2), cc_(C_S2S), 1.0 / 2048.0, None, ALU.mult)
                v.tensor_tensor(cc_(C_VAR1), cc_(C_MU), cc_(C_MU), ALU.mult)
                v.tensor_tensor(cc_(C_VAR1), cc_(C_E2), cc_(C_VAR1), ALU.subtract)
                ins = v.tensor_scalar(cc_(C_A1), cc_(C_VAR1), EPS, None, ALU.add)
                vinc(ins, ("rm1", p))
                v.wait_ge(S["am_sq"], 2 * p + 1)
                v.reciprocal(cc_(C_RS), cc_(C_T1))
                v.reciprocal(cc_(C_RA), cc_(C_A1))
                v.tensor_tensor(cc_(C_VAR2), cc_(C_VAR1), cc_(C_RA), ALU.mult)
                ins = v.tensor_scalar(cc_(C_B2), cc_(C_VAR2), EPS, None, ALU.add)
                vinc(ins, ("rm2", p))
                v.wait_ge(S["am_sq"], 2 * p + 2)
                v.reciprocal(cc_(C_RS2), cc_(C_T2))
                v.tensor_tensor(cc_(C_ALPHA), cc_(C_RS), cc_(C_RS2), ALU.mult)
                v.scalar_tensor_tensor(cc_(C_BETA), cc_(C_MU), -1.0, cc_(C_ALPHA), ALU.mult, ALU.mult)
                v.tensor_scalar(cc_(C_C1), cc_(C_RS), 1.0, None, ALU.add)
                ins = v.scalar_tensor_tensor(cc_(C_C2), cc_(C_MU), -1.0, cc_(C_RS), ALU.mult, ALU.mult)
                vinc(ins, ("row", p))
                if p == 0:
                    v.memset(Fv(FA_O + 32, 32), 0.0)
                    v.memset(Fv(XST_O, 512), 0.0)
                    ins = v.tensor_reduce(Fv(XST_O, 512)[:, 0:1], Fv(FA_O, 64), AX.X, ALU.add)
                    vinc(ins, ("facol",))
                    v.wait_ge(S["pm_bc"], 1)
                    v.tensor_scalar(Fv(THR_O, 8), psB[:, 1024:1032], 0.5 / 4194304.0, 0.5e-5,
                                    ALU.mult, ALU.add)
                    v.memset(Fv(XST_O, 512), 0.0)
                    ins = v.tensor_scalar(Fv(THR_O + 8, 8), Fv(THR_O, 8), -1.0, None, ALU.mult)
                    vinc(ins, ("thr",))
                v.tensor_copy(Fv(ABTIN_O, 128)[:, 0:8], Fv(COL(C_ALPHA), 64)[:, 8 * p:8 * p + 8])
                ins = v.tensor_copy(Fv(ABTIN_O, 128)[:, 8:16], Fv(COL(C_BETA), 64)[:, 8 * p:8 * p + 8])
                vinc(ins, ("abpack", p))
                v.wait_ge(S["pm_bc"], 2 if p == 0 else 5)
                ins = v.tensor_copy(Fv(ABTEV_O, 128)[0:16, :], psB[0:16, 0:128])
                vinc(ins, ("abev", p))
                v.memset(Fv(AROW_O, 1024), 0.0)
                ins = v.memset(Fv(BROW_O, 1024), 0.0)
                vinc(ins, ("abz", p))
                for c in range(NCH):
                    v.wait_ge(S["gm_xt"], 16 * (16 * p + c + 1))
                    v.wait_ge(S["pm_bc"], 4 if p == 0 else 7)
                    xb_ = Fv(XTB_O + (c % 2) * 1024, 1024) if p == 0 else \
                        Fv([XTB_O, XTB_O + 1024, FFWR_O, FFWR_O + 1024][c % 4], 1024)
                    v.tensor_tensor(xb_, xb_, psB[:, 0:1024], ALU.mult)
                    ins = v.tensor_tensor(xb_, xb_, psB[:, 1024:2048], ALU.add)
                    vinc(ins, ("ut", p, c))
                bar(v, "b_ve", 2 + 2 * p)
                for j in range(NPAIR):
                    if j < 8:
                        for s_ in range(2):
                            step = 2 * j + s_
                            v.wait_ge(S["pm_rh"], F_rh(p, step + 1))
                            v.wait_ge(S["sm_tab"], 16 * (16 * p + 2 * j + 2))
                            src = B2(KBF_O if s_ == 0 else VBF_O, 1024)
                            dst = B2(KROT_O if s_ == 0 else VROT_O, 1024)
                            pss = psA[:, 0:1024] if s_ == 0 else psA[:, 1024:2048]
                            ins = v.tensor_tensor(B2(T1B_O, 1024), pss, tab_sin(j), ALU.mult)
                            vinc(ins, ("t1", p, step))
                            v.wait_ge(S["am_evac"], F_evac(p, step + 1))
                            v.tensor_tensor(dst, src, tab_cos(j), ALU.mult)
                            ins = v.tensor_tensor(dst, dst, B2(T1B_O, 1024), ALU.add)
                            vinc(ins, ("rc", p, step))
                    for s_, nko in ((0, NK2_O), (1, NV2_O)):
                        step = 2 * j + s_
                        v.wait_ge(S["am_ksq"], F_ksq(p, step + 1))
                        if j == 0:
                            ins = v.tensor_copy(B2(nko, 1024), B2(KSQ_O, 1024))
                        else:
                            ins = v.tensor_tensor(B2(nko, 1024), B2(KSQ_O, 1024), B2(nko, 1024), ALU.add)
                        vinc(ins, ("nk", p, step))
                    if j >= 2:
                        v.wait_ge(S["gm_pe"], 16 * (16 * p + j - 1))
                    ins = v.tensor_tensor(B2(PBUF_O + (j % 2) * 1024, 1024),
                                          B2(KROT_O, 1024), B2(VROT_O, 1024), ALU.mult)
                    vinc(ins, ("p", p, j))
                    if p == 0 and 3 <= j <= 10:
                        for h2 in range(2):
                            kb = 2 * (j - 3) + h2
                            v.wait_ge(S["sm_x"], 16 * (16 + kb + 1))
                            colb = C_S1A if kb % 2 == 0 else C_S1B
                            ins = v.tensor_reduce(
                                Fv(COL(colb), 64)[:, 8 + kb // 2:9 + kb // 2],
                                Fv(XST_O + (kb % 2) * 1024, 1024), AX.X, ALU.add)
                            vinc(ins, ("s1", 1, kb))
                    if p == 1 and j >= 8:
                        for h in range(2):
                            k = 2 * (j - 8) + h
                            v.wait_ge(S["gm_ff"], 16 * (32 + k + 1))
                            fw = Fv(FFWR_O + (k % 2) * 1024, 1024)
                            v.tensor_scalar(B2(M1B_O, 1024), fw, Fv(THR_O, 8)[:, 0:1], None, ALU.is_gt)
                            ins = v.scalar_tensor_tensor(
                                B2(WCAT_C(k // 2) + 1024 * (k % 2), 1024), fw,
                                Fv(THR_O + 8, 8)[:, 0:1], B2(M1B_O, 1024), ALU.is_ge, ALU.add)
                            vinc(ins, ("ffq", k))
                v.wait_ge(S["pm_nm"], 2 * p + 1)
                ins = v.tensor_copy(Fv(NRMA_O, 1024), psA[:, 0:1024])
                vinc(ins, ("nkc", p))
                v.wait_ge(S["pm_nm"], 2 * p + 2)
                ins = v.tensor_tensor(Fv(NRMB_O, 1024), Fv(NRMA_O, 1024), psA[:, 1024:2048], ALU.mult)
                vinc(ins, ("prod", p))
                v.wait_ge(S["am_sp"], p + 1)
                v.reciprocal(Fv(NRMB_O, 1024), Fv(NRMA_O, 1024))
                ins = v.tensor_copy(B2(WBB_O, 1024), Fv(NRMB_O, 1024))
                vinc(ins, ("wb", p))
                PRET4 = [PRET_O, PRET_O + 1024, KROT_O, VROT_O]
                FFW4 = [FFWR_O, FFWR_O + 1024, XST_O, XST_O + 1024]
                for j in range(NPAIR):
                    v.wait_ge(S["gm_pr"], 16 * (16 * p + j + 1))
                    ins = v.scalar_tensor_tensor(
                        B2(ADUMP_O, 1024), B2(PRET4[j % 4], 1024),
                        0.0, B2(WBB_O, 1024), ALU.bypass, ALU.mult,
                        accum_out=Fv(APART_O, 32)[:, 16 * p + j:16 * p + j + 1])
                    vinc(ins, ("ap", p, j))
                    if p == 1:
                        k = 16 + j
                        v.wait_ge(S["gm_ff"], 16 * (32 + k + 1))
                        fw = Fv(FFW4[k % 4], 1024)
                        v.tensor_scalar(B2(M1B_O, 1024), fw, Fv(THR_O, 8)[:, 0:1], None, ALU.is_gt)
                        ins = v.scalar_tensor_tensor(
                            B2(WCAT_C(k // 2) + 1024 * (k % 2), 1024), fw,
                            Fv(THR_O + 8, 8)[:, 0:1], B2(M1B_O, 1024), ALU.is_ge, ALU.add)
                        vinc(ins, ("ffq", k))
                if p == 1:
                    v.tensor_tensor(Fv(A2_O, 16), Fv(APART_O, 32)[:, 0:16],
                                    Fv(APART_O, 32)[:, 16:32], ALU.add)
                    v.memset(Fv(ABTEV_O, 64), 0.0)
                    v.memset(Fv(ABTEV_O, 64), 0.0)
                    v.wait_ge(S["sm_par"], 16 * 9)
                    for g_ in range(4):
                        ins = v.tensor_tensor(Fv(FA_O + 16 * g_, 16), Fv(A2_O, 16),
                                              Fv(BT16_O + 16 * g_, 16), ALU.mult)
                    vinc(ins, ("a2",))
                    v.wait_ge(S["gm_misc"], 16 * 8)
                    v.tensor_tensor(Fv(FA_O, 64), Fv(FA_O, 64), Fv(BT16_O, 64), ALU.mult)
                    v.memset(Fv(ABTIN_O, 64), 0.0)
                    v.memset(Fv(ABTEV_O, 64), 0.0)
                    v.tensor_tensor(Fv(ABTIN_O, 32), Fv(FA_O, 32), Fv(FA_O + 32, 32), ALU.add)
                    v.memset(Fv(A2_O, 16), 0.0)
                    v.memset(Fv(ABTEV_O, 64), 0.0)
                    v.tensor_tensor(Fv(A2_O, 16), Fv(ABTIN_O, 16), Fv(ABTIN_O + 16, 16), ALU.add)
                    v.memset(Fv(ABTEV_O, 64), 0.0)
                    v.memset(Fv(ABTEV_O, 64), 0.0)
                    ins = v.tensor_scalar(Fv(ACOL_O, 16), Fv(A2_O, 16), 0.5, 0.5, ALU.mult, ALU.add)
                    vinc(ins, ("af",))
                bar(v, "b_ve", 3 + 2 * p)
            ins = v.memset(Fv(BIAST_O, 2048), 0.0)
            vinc(ins, ("btz",))
            v.wait_ge(S["pm_bc"], 8)
            ins = v.tensor_copy(Fv(BIAST_O, 2048), psB[:, 0:2048])
            vinc(ins, ("bT",))
            for stt in range(NST):
                for h in range(2):
                    k = 2 * stt + h
                    v.wait_ge(S["sm_x2"], 16 * (k + 1))
                    if k >= 2:
                        v.wait_ge(S["sm_out"], 16 * (k - 1))
                    ob = Fv(OUTR_O + (k % 2) * 1024, 1024)
                    ins = v.scalar_tensor_tensor(
                        ob, Fv(X2R_O + (k % 2) * 1024, 1024),
                        Fv(COL(C_C1), 64)[:, stt:stt + 1],
                        Fv(BIAST_O, 2048)[:, 1024 * h:1024 * h + 1024], ALU.mult, ALU.add)
                    vinc(ins, ("w1", k))
                    v.wait_ge(S["pm_ph2"], stt + 1)
                    ins = v.scalar_tensor_tensor(
                        ob, ob, Fv(COL(C_C2), 64)[:, stt:stt + 1],
                        (psA if stt % 2 == 0 else psB)[:, 1024 * h:1024 * h + 1024],
                        ALU.add, ALU.add)
                    vinc(ins, ("o", k))
            bar(v, "b_ve", 6)

        @block.scalar
        def _(s):
            nAff = [0]
            nEv = [0]
            nKq = [0]
            bar(s, "b_ac", 1)
            s.wait_ge(S["sm_par"], 16 * 8)
            for p in range(2):
                if p == 0:
                    for k in range(16):
                        t, h = k // 2, k % 2
                        s.wait_ge(S["sm_x"], 16 * (k + 1))
                        col = C_S2A if h == 0 else C_S2B
                        s.activation(B2(DUMPS_O, 1024), Fv(XST_O + (k % 2) * 1024, 1024),
                                     AF.Square,
                                     accum_out=Fv(COL(col), 64)[:, t:t + 1]).then_inc(S["am_st"], 1)
                s.wait_ge(S["vm"], M[("rm1", p)])
                s.activation(Fv(COL(C_T1), 64), Fv(COL(C_A1), 64), AF.Sqrt).then_inc(S["am_sq"], 1)
                s.wait_ge(S["vm"], M[("rm2", p)])
                s.activation(Fv(COL(C_T2), 64), Fv(COL(C_B2), 64), AF.Sqrt).then_inc(S["am_sq"], 1)
                for c in range(NCH):
                    s.wait_ge(S["vm"], M[("ut", p, c)])
                    xb_ = Fv(XTB_O + (c % 2) * 1024, 1024) if p == 0 else \
                        Fv([XTB_O, XTB_O + 1024, FFWR_O, FFWR_O + 1024][c % 4], 1024)
                    s.activation(B3(XN2_O, 16, 1024)[:, c, :], xb_, AF.Identity,
                                 bias=Fv(BCOL_O, 16)[:, c:c + 1],
                                 scale=Fv(GCOL_O, 16)[:, c:c + 1]).then_inc(S["am_aff"], 1)
                    if c >= 4:
                        s.wait_ge(S["sm_fe"], 16 * 8 * (16 * p + c - 3))
                    s.activation(B2(FINB_O + (c % 4) * 1024, 1024), xb_, AF.Identity,
                                 bias=Fv(LNB_O, 16)[:, c:c + 1],
                                 scale=Fv(LNG_O, 16)[:, c:c + 1]).then_inc(S["am_aff"], 1)
                    nAff[0] += 2
                    assert nAff[0] == F_aff(p, 2 * c + 2)
                bar(s, "b_ac", 2 + 2 * p)
                for j in range(NPAIR):
                    for s_ in range(2):
                        step = 2 * j + s_
                        q = (16 if s_ == 0 else 32) + j
                        s.wait_ge(S["pm_mm"], F_mm(p, step + 1))
                        if j < 8:
                            dst = B2(KBF_O if s_ == 0 else VBF_O, 1024)
                            if j >= 1:
                                s.wait_ge(S["vm"], M[("rc", p, 2 * (j - 1) + s_)])
                        else:
                            dst = B2(KROT_O if s_ == 0 else VROT_O, 1024)
                            s.wait_ge(S["vm"], M[("p", p, j - 1)])
                        pss = psA[:, 0:1024] if s_ == 0 else psA[:, 1024:2048]
                        s.activation(dst, pss, AF.Identity,
                                     bias=Fv(BIN_O, 48)[:, q:q + 1]).then_inc(S["am_evac"], 1)
                        nEv[0] += 1
                        assert nEv[0] == F_evac(p, step + 1)
                    for s_ in range(2):
                        step = 2 * j + s_
                        src = B2(KROT_O if s_ == 0 else VROT_O, 1024)
                        if j < 8:
                            s.wait_ge(S["vm"], M[("rc", p, step)])
                        if s_ == 1:
                            s.wait_ge(S["vm"], M[("nk", p, 2 * j)])
                        s.activation(B2(KSQ_O, 1024), src, AF.Square).then_inc(S["am_ksq"], 1)
                        nKq[0] += 1
                        assert nKq[0] == F_ksq(p, step + 1)
                    if p == 0 and 3 <= j <= 10:
                        for h2 in range(2):
                            kb = 2 * (j - 3) + h2
                            s.wait_ge(S["sm_x"], 16 * (16 + kb + 1))
                            colb = C_S2A if kb % 2 == 0 else C_S2B
                            s.activation(B2(MB_O, 1024), Fv(XST_O + (kb % 2) * 1024, 1024),
                                         AF.Square,
                                         accum_out=Fv(COL(colb), 64)[:, 8 + kb // 2:9 + kb // 2]
                                         ).then_inc(S["am_st"], 1)
                    if p == 1 and j >= 8:
                        for h in range(2):
                            k = 2 * (j - 8) + h
                            s.wait_ge(S["vm"], M[("ffq", k)])
                            wslot = B2(WCAT_C(k // 2) + 1024 * (k % 2), 1024)
                            s.activation(wslot, wslot, AF.Identity,
                                         bias=Fv(THR_O + 16, 8)[:, 0:1]).then_inc(S["am_ffq"], 1)
                s.wait_ge(S["vm"], M[("prod", p)])
                s.activation(Fv(NRMA_O, 1024), Fv(NRMB_O, 1024), AF.Sqrt).then_inc(S["am_sp"], 1)
                if p == 1:
                    pass
                for k in range(NQ):
                    step = 32 + k
                    s.wait_ge(S["pm_mm"], F_mm(p, step + 1))
                    if k >= 2:
                        s.wait_ge(S["sm_qe"], 16 * 8 * (16 * p + k - 1))
                    pss = psA[:, 0:1024] if step % 2 == 0 else psA[:, 1024:2048]
                    s.activation(B2(QBF_O + (k % 2) * 1024, 1024), pss, AF.Identity,
                                 bias=Fv(BIN_O, 48)[:, k:k + 1]).then_inc(S["am_evac"], 1)
                    nEv[0] += 1
                if p == 1:
                    for kf in range(16, 32):
                        s.wait_ge(S["vm"], M[("ffq", kf)])
                        wslot = B2(WCAT_C(kf // 2) + 1024 * (kf % 2), 1024)
                        s.activation(wslot, wslot, AF.Identity,
                                     bias=Fv(THR_O + 16, 8)[:, 0:1]).then_inc(S["am_ffq"], 1)
                if p == 1:
                    s.wait_ge(S["vm"], M[("af",)])
                    for idx, j in enumerate(range(10, 16)):
                        s.wait_ge(S["gm_wc"], 16 * (idx + 1))
                        s.activation(B2(WCAT_C(16 + j), 2048), B2(WCAT_C(16 + j), 2048),
                                     AF.Identity,
                                     scale=Fv(ACOL_O, 16)[:, j:j + 1]).then_inc(S["am_ffq"], 1)
                bar(s, "b_ac", 3 + 2 * p)
            for stt in range(NST):
                for h in range(2):
                    k = 2 * stt + h
                    if k >= 2:
                        s.wait_ge(S["vm"], M[("w1", k - 2)])
                    s.dma_start(out=Fv(X2R_O + (k % 2) * 1024, 1024),
                                in_=x_d[128 * stt:128 * (stt + 1), 1024 * h:1024 * h + 1024]
                                ).then_inc(S["sm_x2"], 16)
                if stt == 0:
                    for idx, j in enumerate(range(10)):
                        s.wait_ge(S["gm_wc"], 16 * (7 + idx))
                        s.activation(B2(WCAT_C(16 + j), 2048), B2(WCAT_C(16 + j), 2048),
                                     AF.Identity,
                                     scale=Fv(ACOL_O, 16)[:, j:j + 1]).then_inc(S["am_ffq"], 1)
                if stt >= 1:
                    for h in range(2):
                        k = 2 * (stt - 1) + h
                        s.wait_ge(S["vm"], M[("o", k)])
                        s.dma_start(out=out_d[128 * (stt - 1):128 * stt, 1024 * h:1024 * h + 1024],
                                    in_=Fv(OUTR_O + (k % 2) * 1024, 1024)).then_inc(S["sm_out"], 16)
            for h in range(2):
                k = 30 + h
                s.wait_ge(S["vm"], M[("o", k)])
                s.dma_start(out=out_d[128 * 15:2048, 1024 * h:1024 * h + 1024],
                            in_=Fv(OUTR_O + (k % 2) * 1024, 1024)).then_inc(S["sm_out"], 16)
            bar(s, "b_ac", 6)

        @block.tensor
        def _(te):
            pmm = [0]
            pbc = [0]

            def bcinc(ins):
                ins.then_inc(S["pm_bc"], 1)
                pbc[0] += 1
                return pbc[0]

            bar(te, "b_pe", 1)
            te.wait_ge(S["sm_par"], 16 * 8)
            for p in range(2):
                if p == 0:
                    te.wait_ge(S["vm"], M[("facol",)])
                    assert bcinc(te.matmul(psB[:, 1024:1536], Fv(ONESF_O, 128),
                                           Fv(XST_O, 512),
                                           start=True, stop=True)) == 1
                te.wait_ge(S["vm"], M[("abpack", p)])
                assert bcinc(te.transpose(psB[:, 0:128], Fv(ABTIN_O, 128),
                                          Fv(IDENT_O, 128))) == (2 if p == 0 else 5)
                te.wait_ge(S["gm_misc"], 16 * (3 + 3 * p))
                for half in range(2):
                    ins = te.matmul(psB[:, 512 * half:512 * half + 512], Fv(ONESF_O, 128),
                                    Fv(AROW_O, 1024)[:, 512 * half:512 * half + 512],
                                    start=True, stop=True)
                assert bcinc(ins) == (3 if p == 0 else 6)
                for half in range(2):
                    ins = te.matmul(psB[:, 1024 + 512 * half:1024 + 512 * half + 512],
                                    Fv(ONESF_O, 128),
                                    Fv(BROW_O, 1024)[:, 512 * half:512 * half + 512],
                                    start=True, stop=True)
                assert bcinc(ins) == (4 if p == 0 else 7)
                bar(te, "b_pe", 2 + 2 * p)
                for i in range(48):
                    te.wait_ge(S["gm_w"], 16 * (48 * p + i + 1))
                    if i >= 2:
                        prev = i - 2
                        if prev < 16:
                            te.wait_ge(S["vm"], M[("t1", p, prev)])
                        elif i == 32:
                            te.wait_ge(S["vm"], M[("nkc", p)])
                        elif i == 33:
                            te.wait_ge(S["vm"], M[("prod", p)])
                        else:
                            te.wait_ge(S["am_evac"], F_evac(p, prev + 1))
                    pss = psA[:, 0:1024] if i % 2 == 0 else psA[:, 1024:2048]
                    wsl = B3(WSLAB_O + (i % 2) * 2048, 16, 128)
                    xn2 = B3(XN2_O, 16, 1024)
                    for c in range(NCH):
                        for g_ in range(2):
                            ins = te.matmul(pss[:, 512 * g_:512 * g_ + 512], wsl[:, c, :],
                                            xn2[:, c, 512 * g_:512 * g_ + 512],
                                            start=(c == 0), stop=(c == NCH - 1))
                    ins.then_inc(S["pm_mm"], 1)
                    pmm[0] += 1
                    assert pmm[0] == F_mm(p, i + 1)
                    if i < 32 and i % 2 == 1 and i // 2 < 8:
                        j = i // 2
                        for s_ in range(2):
                            step = 2 * j + s_
                            te.wait_ge(S["am_evac"], F_evac(p, step + 1))
                            src = B2(KBF_O if s_ == 0 else VBF_O, 1024)
                            pss2 = psA[:, 0:1024] if s_ == 0 else psA[:, 1024:2048]
                            for g_ in range(2):
                                ins = te.matmul(pss2[:, 512 * g_:512 * g_ + 512], B2(PSW_O, 128),
                                                src[:, 512 * g_:512 * g_ + 512],
                                                start=True, stop=True)
                            ins.then_inc(S["pm_rh"], 1)
                    if i == 31:
                        te.wait_ge(S["vm"], M[("nk", p, 31)])
                        te.wait_ge(S["am_evac"], F_evac(p, 31))
                        for g_ in range(2):
                            ins = te.matmul(psA[:, 512 * g_:512 * g_ + 512], B2(ONESB_O, 128),
                                            B2(NK2_O, 1024)[:, 512 * g_:512 * g_ + 512],
                                            start=True, stop=True)
                        ins.then_inc(S["pm_nm"], 1)
                        te.wait_ge(S["am_evac"], F_evac(p, 32))
                        for g_ in range(2):
                            ins = te.matmul(psA[:, 1024 + 512 * g_:1024 + 512 * g_ + 512],
                                            B2(ONESB_O, 128),
                                            B2(NV2_O, 1024)[:, 512 * g_:512 * g_ + 512],
                                            start=True, stop=True)
                        ins.then_inc(S["pm_nm"], 1)
                bar(te, "b_pe", 3 + 2 * p)
            te.wait_ge(S["sm_par"], 16 * 11)
            for g_ in range(4):
                ins = te.matmul(psB[:, 512 * g_:512 * g_ + 512], Fv(ONESF_O, 128),
                                Fv(BIAST_O, 2048)[:, 512 * g_:512 * g_ + 512],
                                start=True, stop=True)
            assert bcinc(ins) == 8
            d_seq = list(range(16)) + list(range(26, 32)) + list(range(16, 26))
            wc_idx = {26 + j: j + 1 for j in range(6)}
            wc_idx.update({16 + j: 7 + j for j in range(10)})
            for stt in range(NST):
                te.wait_ge(S["sm_sr"], 16 * (2 * stt + 2))
                if stt >= 2:
                    te.wait_ge(S["vm"], M[("o", 2 * (stt - 2) + 1)])
                if stt < 2:
                    te.wait_ge(S["vm"], M[("bT",)])
                pss = psA if stt % 2 == 0 else psB
                fsl = B3(SLABF_O + (stt % 4) * 2048, 16, 128)
                ysl = B3(SLABQ_O + (stt % 4) * 2048, 16, 128)
                if stt == 1:
                    te.wait_ge(S["am_ffq"], 48)
                for di, d in enumerate(d_seq):
                    if d >= 16 and stt == 0:
                        te.wait_ge(S["am_ffq"], 32 + wc_idx[d])
                    statn = fsl[:, d, :] if d < 16 else ysl[:, d - 16, :]
                    for g_ in range(4):
                        ins = te.matmul(pss[:, 512 * g_:512 * g_ + 512], statn,
                                        B2(WCAT_C(d), 2048)[:, 512 * g_:512 * g_ + 512],
                                        start=(di == 0), stop=(di == 31))
                ins.then_inc(S["pm_ph2"], 1)
            bar(te, "b_pe", 6)

        @block.gpsimd
        def _(g):
            gmisc = [0]

            def gmi(ins):
                ins.then_inc(S["gm_misc"], 16)
                gmisc[0] += 1
                return gmisc[0]

            bar(g, "b_gp", 1)
            for p in range(2):
                if p == 0:
                    for k in range(32):
                        if k >= 2:
                            g.wait_ge(S["vm"], M[("fa", k - 2)])
                        g.dma_start(out=Fv(FFWR_O + (k % 2) * 1024, 1024),
                                    in_=ffw_d[k // 2][:, 1024 * (k % 2):1024 * (k % 2) + 1024]
                                    ).then_inc(S["gm_ff"], 16)
                g.wait_ge(S["vm"], M[("abev", p)])
                g.wait_ge(S["vm"], M[("abz", p)])
                assert gmi(g.dma_start(out=abd, in_=Fv(ABTEV_O, 128)[0:16, :])) == 1 + 3 * p
                g.wait_ge(S["gm_misc"], 16 * (1 + 3 * p))
                assert gmi(g.dma_start(out=Fv(AROW_O, 1024)[0:1, :],
                                       in_=abd[0:8].rearrange("c p -> (c p)")[None, :])) == 2 + 3 * p
                assert gmi(g.dma_start(out=Fv(BROW_O, 1024)[0:1, :],
                                       in_=abd[8:16].rearrange("c p -> (c p)")[None, :])) == 3 + 3 * p
                for c in range(NCH):
                    if p == 0:
                        if c >= 2:
                            g.wait_ge(S["am_aff"], F_aff(p, 2 * (c - 1)))
                        g.dma_start(out=Fv(XTB_O + (c % 2) * 1024, 1024),
                                    in_=xt_d[c][:, 0:TH]).then_inc(S["gm_xt"], 16)
                    elif c >= 4:
                        g.wait_ge(S["am_aff"], F_aff(1, 2 * (c - 4) + 2))
                        g.dma_start(out=Fv([XTB_O, XTB_O + 1024, FFWR_O,
                                            FFWR_O + 1024][c % 4], 1024),
                                    in_=xt_d[c][:, TH:2 * TH]).then_inc(S["gm_xt"], 16)
                for i0 in range(2):
                    g.dma_start(out=B3(WSLAB_O + (i0 % 2) * 2048, 16, 128),
                                in_=win_d[(16, 32)[i0]]).then_inc(S["gm_w"], 16)
                bar(g, "b_gp", 2 + 2 * p)
                fo_list = []
                for j in range(NPAIR):
                    fo_list += [16 + j, 32 + j]
                fo_list += list(range(16))

                def wload(i):
                    if i >= 2:
                        g.wait_ge(S["pm_mm"], F_mm(p, i - 1))
                    g.dma_start(out=B3(WSLAB_O + (i % 2) * 2048, 16, 128),
                                in_=win_d[fo_list[i]]).then_inc(S["gm_w"], 16)

                for i in range(48):
                    if i + 2 < 48:
                        wload(i + 2)
                    if i < 32 and i % 2 == 1:
                        j = i // 2
                        if p == 1 and 7 <= j <= 14:
                            for h in range(2):
                                k = 2 * (j - 7) + h
                                if k >= 2:
                                    g.wait_ge(S["vm"], M[("ffq", k - 2)])
                                g.dma_start(out=Fv(FFWR_O + (k % 2) * 1024, 1024),
                                            in_=ffw_d[k // 2][:, 1024 * (k % 2):1024 * (k % 2) + 1024]
                                            ).then_inc(S["gm_ff"], 16)
                        g.wait_ge(S["vm"], M[("p", p, j)])
                        g.dma_start(out=ph[p][j], in_=B2(PBUF_O + (j % 2) * 1024, 1024)
                                    ).then_inc(S["gm_pe"], 16)
                    if i >= 32:
                        jr = i - 32
                        PRET4 = [PRET_O, PRET_O + 1024, KROT_O, VROT_O]
                        FFW4 = [FFWR_O, FFWR_O + 1024, XST_O, XST_O + 1024]
                        if p == 1:
                            k = 16 + jr
                            if k - 4 >= 16:
                                g.wait_ge(S["vm"], M[("ffq", k - 4)])
                            else:
                                g.wait_ge(S["vm"], M[("ffq", k - 2)])
                            if FFW4[k % 4] in (XST_O, XST_O + 1024) and k < 20:
                                g.wait_ge(S["vm"], M[("wb", p)])
                            g.dma_start(out=Fv(FFW4[k % 4], 1024),
                                        in_=ffw_d[k // 2][:, 1024 * (k % 2):1024 * (k % 2) + 1024]
                                        ).then_inc(S["gm_ff"], 16)
                        g.wait_ge(S["gm_pe"], 16 * (16 * p + jr + 1))
                        if jr >= 4:
                            g.wait_ge(S["vm"], M[("ap", p, jr - 4)])
                        if jr in (2, 3):
                            g.wait_ge(S["vm"], M[("p", p, 15)])
                        g.dma_start(out=B2(PRET4[jr % 4], 1024), in_=ph[p][jr]
                                    ).then_inc(S["gm_pr"], 16)
                if p == 0:
                    for j in range(10, 16):
                        g.dma_start(out=B2(WCAT_C(16 + j), 2048), in_=wout_d[j]
                                    ).then_inc(S["gm_wc"], 16)
                    for c in range(2):
                        g.dma_start(out=Fv(XTB_O + c * 1024, 1024),
                                    in_=xt_d[c][:, TH:2 * TH]).then_inc(S["gm_xt"], 16)
                    g.wait_ge(S["vm"], M[("fa", 31)])
                    for c in range(2, 4):
                        g.dma_start(out=Fv(FFWR_O + (c - 2) * 1024, 1024),
                                    in_=xt_d[c][:, TH:2 * TH]).then_inc(S["gm_xt"], 16)
                if p == 1:
                    g.wait_ge(S["vm"], M[("a2",)])
                    assert gmi(g.dma_start(out=ar_in, in_=Fv(FA_O, 64))) == 7
                    g.wait_ge(S["gm_misc"], 16 * 7)
                    g.collective_compute(
                        "AllReduce", ALU.add,
                        replica_groups=[[0, 1, 2, 3, 4, 5, 6, 7]],
                        ins=[ar_in[:, :]], outs=[ar_out[:, :]],
                    ).then_inc(S["cc"], 1)
                    g.wait_ge(S["cc"], 1)
                    assert gmi(g.dma_start(out=Fv(FA_O, 64), in_=ar_out)) == 8
                bar(g, "b_gp", 3 + 2 * p)
            g.wait_ge(S["vm"], M[("ap", 1, 15)])
            g.wait_ge(S["vm"], M[("ffq", 31)])
            g.wait_ge(S["gm_pe"], 16 * 32)
            for d in range(10):
                g.dma_start(out=B2(WCAT_C(16 + d), 2048), in_=wout_d[d]
                            ).then_inc(S["gm_wc"], 16)
            bar(g, "b_gp", 6)

        @block.sync
        def _(sy):
            for dst, src in ((Fv(GCOL_O, 16), gcol_d), (Fv(BCOL_O, 16), bcol_d),
                             (Fv(LNG_O, 16), lng_d), (Fv(LNB_O, 16), lnb_d),
                             (Fv(BIN_O, 48), bin_d), (B2(PSW_O, 128), psw_d),
                             (B2(ONESB_O, 128), onesb_d), (Fv(IDENT_O, 128), ident_d),
                             (Fv(ABTEV_O + 128, 64) if False else None, None)):
                if dst is None:
                    continue
                sy.dma_start(out=dst, in_=src).then_inc(S["sm_par"], 16)
            sy.dma_start(out=Fv(BT16_O, 64), in_=armask_d).then_inc(S["sm_par"], 16)
            bar(sy, "b_sy", 1)
            def xstload(p_, k):
                gt, h = 8 * p_ + k // 2, k % 2
                sy.dma_start(out=Fv(XST_O + (k % 2) * 1024, 1024),
                             in_=x_d[128 * gt:128 * (gt + 1), 1024 * h:1024 * h + 1024]
                             ).then_inc(S["sm_x"], 16)

            for p in range(2):
                if p == 0:
                    for k in range(16):
                        if k >= 2:
                            sy.wait_ge(S["vm"], M[("s1", 0, k - 2)])
                            sy.wait_ge(S["am_st"], k - 1)
                        xstload(0, k)
                for c in range(NCH):
                    sy.wait_ge(S["am_aff"], F_aff(p, 2 * c + 2))
                    for t in range(8):
                        sy.dma_start(out=fh[p][t, :, c, :],
                                     in_=B2(FINB_O + (c % 4) * 1024, 1024)[:, 128 * t:128 * (t + 1)]
                                     ).then_inc(S["sm_fe"], 16)
                bar(sy, "b_sy", 2 + 2 * p)
                sy.wait_ge(S["sm_fe"], 16 * 8 * 16 * (p + 1))
                for j in range(8):
                    if j >= 2:
                        sy.wait_ge(S["vm"], M[("rc", p, 2 * (j - 2) + 1)])
                    sy.dma_start(out=tab_cos(j), in_=cost_d[p][j]).then_inc(S["sm_tab"], 16)
                    sy.dma_start(out=tab_sin(j), in_=sint_d[p][j]).then_inc(S["sm_tab"], 16)
                    if p == 0 and j == 1:
                        xstload(1, 0)
                        xstload(1, 1)
                    if p == 0 and j >= 5:
                        for kb in (2 * (j - 4), 2 * (j - 4) + 1):
                            sy.wait_ge(S["vm"], M[("s1", 1, kb - 2)])
                            sy.wait_ge(S["am_st"], 16 + kb - 1)
                            xstload(1, kb)
                if p == 0:
                    for kb in range(8, 16):
                        sy.wait_ge(S["vm"], M[("s1", 1, kb - 2)])
                        sy.wait_ge(S["am_st"], 16 + kb - 1)
                        xstload(1, kb)
                for k in range(NQ):
                    sy.wait_ge(S["am_evac"], F_evac(p, 32 + k + 1))
                    for t in range(8):
                        sy.dma_start(out=qh[p][t, :, k, :],
                                     in_=B2(QBF_O + (k % 2) * 1024, 1024)[:, 128 * t:128 * (t + 1)]
                                     ).then_inc(S["sm_qe"], 16)
                bar(sy, "b_sy", 3 + 2 * p)
            sy.wait_ge(S["vm"], M[("btz",)])
            sy.dma_start(out=Fv(BIAST_O, 2048)[0:1, :], in_=boutr_d).then_inc(S["sm_par"], 16)
            sy.dma_start(out=Fv(BIAST_O, 2048)[1:2, :], in_=fbr_d).then_inc(S["sm_par"], 16)
            sy.wait_ge(S["sm_fe"], 16 * 8 * 32)
            sy.wait_ge(S["sm_qe"], 16 * 8 * 32)
            for stt in range(NST):
                p, stl = stt // 8, stt % 8
                if stt >= 4:
                    sy.wait_ge(S["pm_ph2"], stt - 3)
                sy.dma_start(out=B3(SLABF_O + (stt % 4) * 2048, 16, 128),
                             in_=fh[p][stl]).then_inc(S["sm_sr"], 16)
                sy.dma_start(out=B3(SLABQ_O + (stt % 4) * 2048, 16, 128),
                             in_=qh[p][stl]).then_inc(S["sm_sr"], 16)
            bar(sy, "b_sy", 6)
            sy.wait_ge(S["sm_out"], 16 * 32)

    es.close()
    return nc


_NC_CACHE = [None]


def kernel(L0_attn_X, L0_attn_W_in, L0_attn_b_in, L0_attn_W_out, L0_attn_b_out,
           L0_attn_gamma, L0_attn_beta, L0_ffn_weight, L0_ffn_bias,
           L0_ffn_ln_gamma, L0_ffn_ln_beta):
    X = np.ascontiguousarray(np.asarray(L0_attn_X, dtype=np.float32)).reshape(16384, 2048)
    W_in = np.asarray(L0_attn_W_in, dtype=np.float32)
    W_out = np.asarray(L0_attn_W_out, dtype=np.float32)
    ffw = np.asarray(L0_ffn_weight, dtype=np.float32)

    win_r = np.ascontiguousarray(
        W_in.reshape(16, 128, 48, 128).transpose(2, 1, 0, 3))
    wout_r = np.ascontiguousarray(W_out.reshape(16, 128, 2048))
    ffwT = np.ascontiguousarray(ffw.T).reshape(16, 128, 2048)

    def colform(vv, w):
        return np.ascontiguousarray(np.asarray(vv, dtype=np.float32).reshape(w, 128).T)

    gcol = colform(L0_attn_gamma, 16)
    bcol = colform(L0_attn_beta, 16)
    lngcol = colform(L0_ffn_ln_gamma, 16)
    lnbcol = colform(L0_ffn_ln_beta, 16)
    bincol = colform(L0_attn_b_in, 48)
    boutrow = np.ascontiguousarray(np.asarray(L0_attn_b_out, dtype=np.float32).reshape(1, 2048))
    fbrow = np.ascontiguousarray(np.asarray(L0_ffn_bias, dtype=np.float32).reshape(1, 2048))

    rot = 1024
    inv = (np.float32(1.0) / (np.float32(10000.0) **
                              (np.arange(0, rot, 2, dtype=np.float32) / np.float32(rot)))
           ).astype(np.float32)
    pos = np.arange(4096, dtype=np.float32)
    fr = np.repeat((pos[:, None] * inv[None, :]).astype(np.float32), 2, axis=1)
    cosf = np.cos(fr).astype(np.float32).T
    sinf = np.sin(fr).astype(np.float32).T

    psw = np.zeros((128, 128), dtype=np.float32)
    for i in range(64):
        psw[2 * i + 1, 2 * i] = -1.0
        psw[2 * i, 2 * i + 1] = 1.0
    psw_bf = psw.astype(BFH)
    onesb = np.ones((128, 128), dtype=BFH)
    armasks = []
    for c in range(NCORE):
        m_ = np.zeros((128, 64), dtype=np.float32)
        m_[:, 16 * (c // 2):16 * (c // 2) + 16] = 1.0
        armasks.append(m_)
    ident = np.eye(128, dtype=np.float32)

    if _NC_CACHE[0] is None:
        _NC_CACHE[0] = build_nc()
    nc = _NC_CACHE[0]

    in_maps = []
    for c in range(NCORE):
        b, half = c // 2, c % 2
        r0 = b * 4096 + half * 2048
        xc = np.ascontiguousarray(X[r0:r0 + 2048])
        xtc = np.ascontiguousarray(xc.T).reshape(16, 128, 2048)
        s0 = half * 2048
        ctp = np.ascontiguousarray(
            cosf[:, s0:s0 + 2048].reshape(8, 128, 2, 1024).transpose(2, 0, 1, 3)).astype(BFH)
        stp = np.ascontiguousarray(
            sinf[:, s0:s0 + 2048].reshape(8, 128, 2, 1024).transpose(2, 0, 1, 3)).astype(BFH)
        in_maps.append({
            "x": xc, "xt": xtc, "win": win_r, "wout": wout_r, "ffw": ffwT,
            "cost": ctp, "sint": stp,
            "gcol": gcol, "bcol": bcol, "lngcol": lngcol, "lnbcol": lnbcol,
            "bincol": bincol, "boutrow": boutrow, "fbrow": fbrow,
            "pairswap": psw_bf, "onesb": onesb, "ident": ident, "armask": armasks[c],
        })

    res = run_bass_kernel_spmd(nc, in_maps, core_ids=list(range(NCORE)))
    out = np.empty((16384, 2048), dtype=np.float32)
    for c in range(NCORE):
        b, half = c // 2, c % 2
        r0 = b * 4096 + half * 2048
        out[r0:r0 + 2048] = res.results[c]["out"]
    return out.reshape(4, 4096, 2048)


# version 25
# speedup vs baseline: 1.0302x; 1.0263x over previous
import numpy as np
import ml_dtypes
import contextlib
import concourse.bass as bass
import concourse.mybir as mybir
from concourse.bass_utils import run_bass_kernel_spmd

F32 = mybir.dt.float32
BF16 = mybir.dt.bfloat16
AF = mybir.ActivationFunctionType
ALU = mybir.AluOpType
AX = mybir.AxisListType
BFH = ml_dtypes.bfloat16

TH = 1024
NCH = 16
NPAIR = 16
NQ = 16
NST = 16
EPS = 1e-5
NCORE = 8

XN2_O = 0
SLABF_O = 0
SLABQ_O = 8192
WSLAB_O = 16384
QBF_O = 20480
DUMPS_O = 20480
WCAT_O = 22528


def WCAT_C(d):
    return WCAT_O + 2048 * d


TABC_O = 55296
FINB_O = 55296
KBF_O = 59392
VBF_O = 60416
KROT_O = 61440
VROT_O = 62464
KSQ_O = 63488
T1B_O = 64512
NK2_O = 65536
NV2_O = 66560
PBUF_O = 67584
PRET_O = 69632
WBB_O = 71680
M1B_O = 72704
MB_O = 73728
ADUMP_O = KBF_O
PSW_O = 74752
ONESB_O = 74880
BF_TOT = 88064

XST_O = 0
NRMA_O = 0
NRMB_O = 1024
AROW_O = 0
BROW_O = 1024
X2R_O = 0
XTB_O = 2048
OUTR_O = 2048
FFWR_O = 4096
BIAST_O = 4096
COLS_O = 6144


def COL(i):
    return COLS_O + 64 * i


(C_S1A, C_S1B, C_S2A, C_S2B, C_MU, C_E2, C_VAR1, C_A1, C_RA, C_RS, C_VAR2,
 C_B2, C_RS2, C_ALPHA, C_BETA, C_C1, C_C2, C_T1, C_T2, C_S1S, C_S2S) = range(21)
MISC_O = COLS_O + 21 * 64
ABTIN_O = MISC_O
ABTEV_O = MISC_O + 128
BT16_O = MISC_O + 256
THR_O = MISC_O + 384
APART_O = MISC_O + 448
A2_O = MISC_O + 480
ACOL_O = MISC_O + 496
FA_O = MISC_O + 512
ONESF_O = MISC_O + 576
PAR_O = MISC_O + 704
GCOL_O = PAR_O
BCOL_O = PAR_O + 16
LNG_O = PAR_O + 32
LNB_O = PAR_O + 48
BIN_O = PAR_O + 64
BOUTC_O = PAR_O + 112
FBC_O = PAR_O + 128
ONES1_O = PAR_O + 144
IDENT_O = PAR_O + 272
F32_TOT = PAR_O + 400

BARS = ("b_sy", "b_gp", "b_ac", "b_ve", "b_pe")


def build_nc():
    nc = bass.Bass("TRN2", target_bir_lowering=False, detect_race_conditions=False)

    x_d = nc.dram_tensor("x", [2048, 2048], F32, kind="ExternalInput").ap()
    xt_d = nc.dram_tensor("xt", [NCH, 128, 2048], F32, kind="ExternalInput").ap()
    win_d = nc.dram_tensor("win", [48, 128, NCH, 128], F32, kind="ExternalInput").ap()
    wout_d = nc.dram_tensor("wout", [16, 128, 2048], F32, kind="ExternalInput").ap()
    ffw_d = nc.dram_tensor("ffw", [NCH, 128, 2048], F32, kind="ExternalInput").ap()
    cost_d = nc.dram_tensor("cost", [2, 8, 128, TH], BF16, kind="ExternalInput").ap()
    sint_d = nc.dram_tensor("sint", [2, 8, 128, TH], BF16, kind="ExternalInput").ap()
    gcol_d = nc.dram_tensor("gcol", [128, 16], F32, kind="ExternalInput").ap()
    bcol_d = nc.dram_tensor("bcol", [128, 16], F32, kind="ExternalInput").ap()
    lng_d = nc.dram_tensor("lngcol", [128, 16], F32, kind="ExternalInput").ap()
    lnb_d = nc.dram_tensor("lnbcol", [128, 16], F32, kind="ExternalInput").ap()
    bin_d = nc.dram_tensor("bincol", [128, 48], F32, kind="ExternalInput").ap()
    boutr_d = nc.dram_tensor("boutrow", [1, 2048], F32, kind="ExternalInput").ap()
    fbr_d = nc.dram_tensor("fbrow", [1, 2048], F32, kind="ExternalInput").ap()
    psw_d = nc.dram_tensor("pairswap", [128, 128], BF16, kind="ExternalInput").ap()
    armask_d = nc.dram_tensor("armask", [128, 64], F32, kind="ExternalInput").ap()
    onesb_d = nc.dram_tensor("onesb", [128, 128], BF16, kind="ExternalInput").ap()
    ident_d = nc.dram_tensor("ident", [128, 128], F32, kind="ExternalInput").ap()
    out_d = nc.dram_tensor("out", [2048, 2048], F32, kind="ExternalOutput").ap()

    qh = [nc.dram_tensor(f"qh{p}", [8, 128, NCH, 128], BF16).ap() for p in range(2)]
    fh = [nc.dram_tensor(f"fh{p}", [8, 128, NCH, 128], BF16).ap() for p in range(2)]
    ph = [nc.dram_tensor(f"ph{p}", [NPAIR, 128, TH], BF16).ap() for p in range(2)]
    abd = nc.dram_tensor("abd", [16, 128], F32).ap()
    ar_in = nc.dram_tensor("ar_in", [128, 64], F32).ap()
    ar_out = nc.dram_tensor("ar_out", [128, 64], F32, addr_space="Shared").ap()

    es = contextlib.ExitStack()
    hb = es.enter_context(nc.sbuf_tensor("hb", [128, BF_TOT], BF16))
    hf = es.enter_context(nc.sbuf_tensor("hf", [128, F32_TOT], F32))
    psA = es.enter_context(nc.psum_tensor("psA", [128, 2048], F32))
    psB = es.enter_context(nc.psum_tensor("psB", [128, 2048], F32))

    SEMS = list(BARS) + [
        "sm_par", "sm_x", "sm_tab", "sm_fe", "sm_fev", "sm_feg", "sm_qe",
        "sm_x2", "sm_sr", "sm_out",
        "gm_xt", "gm_ff", "gm_w", "gm_pe", "gm_pr", "gm_wc", "gm_misc", "cc",
        "am_st", "am_sq", "am_aff", "am_ffq", "am_evac", "am_ksq", "am_sp",
        "pm_mm", "pm_rh", "pm_nm", "pm_bc", "pm_ph2", "vm"]
    S = {n: es.enter_context(nc.semaphore(n)) for n in SEMS}

    M = {}
    vmc = [0]

    def vinc(ins, key=None):
        ins.then_inc(S["vm"], 1)
        vmc[0] += 1
        if key is not None:
            M[key] = vmc[0]
        return vmc[0]

    def B2(off, n):
        return hb[:, off:off + n]

    def B3(off, c, n):
        return hb[:, off:off + c * n].rearrange("p (c n) -> p c n", n=n)

    def Fv(off, n):
        return hf[:, off:off + n]

    def F_mm(p, k):
        return 48 * p + k

    def F_rh(p, k):
        return 16 * p + k

    def F_evac(p, k):
        return 48 * p + k

    def F_ksq(p, k):
        return 32 * p + k

    def F_aff(p, k):
        return 32 * p + k

    def tab_cos(j):
        return B2(TABC_O + (j % 2) * 2048, 1024)

    def tab_sin(j):
        return B2(TABC_O + (j % 2) * 2048 + 1024, 1024)

    def bar(eng, own, k):
        eng.sem_inc(S[own], 1)
        for n in BARS:
            if n != own:
                eng.wait_ge(S[n], k)

    with nc.Block() as block:

        @block.vector
        def _(v):
            v.memset(Fv(COLS_O, PAR_O - COLS_O), 1.0)
            v.memset(Fv(THR_O + 16, 8), -1.0)
            v.memset(Fv(THR_O + 32, 8), 0.0)
            v.memset(Fv(ONESF_O, 128), 1.0)
            bar(v, "b_ve", 1)
            for p in range(2):
                if p == 0:
                    for k in range(16):
                        t, h = k // 2, k % 2
                        v.wait_ge(S["sm_x"], 16 * (k + 1))
                        col = C_S1A if h == 0 else C_S1B
                        ins = v.tensor_reduce(Fv(COL(col), 64)[:, t:t + 1],
                                              Fv(XST_O + (k % 2) * 1024, 1024), AX.X, ALU.add)
                        vinc(ins, ("s1", 0, k))
                if p == 0:
                    for k in range(32):
                        v.wait_ge(S["gm_ff"], 16 * (k + 1))
                        ins = v.tensor_reduce(Fv(FA_O, 64)[:, k:k + 1],
                                              Fv(FFWR_O + (k % 2) * 1024, 1024), AX.X, ALU.add,
                                              apply_absolute_value=True)
                        vinc(ins, ("fa", k))
                v.wait_ge(S["am_st"], 16 * p + 16)

                def cc_(i):
                    return Fv(COL(i), 64)

                v.tensor_tensor(cc_(C_S1S), cc_(C_S1A), cc_(C_S1B), ALU.add)
                v.tensor_tensor(cc_(C_S2S), cc_(C_S2A), cc_(C_S2B), ALU.add)
                v.tensor_scalar(cc_(C_MU), cc_(C_S1S), 1.0 / 2048.0, None, ALU.mult)
                v.tensor_scalar(cc_(C_E2), cc_(C_S2S), 1.0 / 2048.0, None, ALU.mult)
                v.tensor_tensor(cc_(C_VAR1), cc_(C_MU), cc_(C_MU), ALU.mult)
                v.tensor_tensor(cc_(C_VAR1), cc_(C_E2), cc_(C_VAR1), ALU.subtract)
                ins = v.tensor_scalar(cc_(C_A1), cc_(C_VAR1), EPS, None, ALU.add)
                vinc(ins, ("rm1", p))
                v.wait_ge(S["am_sq"], 2 * p + 1)
                v.reciprocal(cc_(C_RS), cc_(C_T1))
                v.reciprocal(cc_(C_RA), cc_(C_A1))
                v.tensor_tensor(cc_(C_VAR2), cc_(C_VAR1), cc_(C_RA), ALU.mult)
                ins = v.tensor_scalar(cc_(C_B2), cc_(C_VAR2), EPS, None, ALU.add)
                vinc(ins, ("rm2", p))
                v.wait_ge(S["am_sq"], 2 * p + 2)
                v.reciprocal(cc_(C_RS2), cc_(C_T2))
                v.tensor_tensor(cc_(C_ALPHA), cc_(C_RS), cc_(C_RS2), ALU.mult)
                v.scalar_tensor_tensor(cc_(C_BETA), cc_(C_MU), -1.0, cc_(C_ALPHA), ALU.mult, ALU.mult)
                v.tensor_scalar(cc_(C_C1), cc_(C_RS), 1.0, None, ALU.add)
                ins = v.scalar_tensor_tensor(cc_(C_C2), cc_(C_MU), -1.0, cc_(C_RS), ALU.mult, ALU.mult)
                vinc(ins, ("row", p))
                if p == 0:
                    v.memset(Fv(FA_O + 32, 32), 0.0)
                    v.memset(Fv(XST_O, 512), 0.0)
                    ins = v.tensor_reduce(Fv(XST_O, 512)[:, 0:1], Fv(FA_O, 64), AX.X, ALU.add)
                    vinc(ins, ("facol",))
                    v.wait_ge(S["pm_bc"], 1)
                    v.tensor_scalar(Fv(THR_O, 8), psB[:, 1024:1032], 0.5 / 4194304.0, 0.5e-5,
                                    ALU.mult, ALU.add)
                    v.memset(Fv(XST_O, 512), 0.0)
                    ins = v.tensor_scalar(Fv(THR_O + 8, 8), Fv(THR_O, 8), -1.0, None, ALU.mult)
                    vinc(ins, ("thr",))
                v.tensor_copy(Fv(ABTIN_O, 128)[:, 0:8], Fv(COL(C_ALPHA), 64)[:, 8 * p:8 * p + 8])
                ins = v.tensor_copy(Fv(ABTIN_O, 128)[:, 8:16], Fv(COL(C_BETA), 64)[:, 8 * p:8 * p + 8])
                vinc(ins, ("abpack", p))
                v.wait_ge(S["pm_bc"], 2 if p == 0 else 5)
                ins = v.tensor_copy(Fv(ABTEV_O, 128)[0:16, :], psB[0:16, 0:128])
                vinc(ins, ("abev", p))
                v.memset(Fv(AROW_O, 1024), 0.0)
                ins = v.memset(Fv(BROW_O, 1024), 0.0)
                vinc(ins, ("abz", p))
                for c in range(NCH):
                    v.wait_ge(S["gm_xt"], 16 * (16 * p + c + 1))
                    v.wait_ge(S["pm_bc"], 4 if p == 0 else 7)
                    xb_ = Fv(XTB_O + (c % 2) * 1024, 1024) if p == 0 else \
                        Fv([XTB_O, XTB_O + 1024, FFWR_O, FFWR_O + 1024][c % 4], 1024)
                    v.tensor_tensor(xb_, xb_, psB[:, 0:1024], ALU.mult)
                    ins = v.tensor_tensor(xb_, xb_, psB[:, 1024:2048], ALU.add)
                    vinc(ins, ("ut", p, c))
                bar(v, "b_ve", 2 + 2 * p)
                for j in range(NPAIR):
                    if j < 8:
                        for s_ in range(2):
                            step = 2 * j + s_
                            v.wait_ge(S["pm_rh"], F_rh(p, step + 1))
                            v.wait_ge(S["sm_tab"], 16 * (16 * p + 2 * j + 2))
                            src = B2(KBF_O if s_ == 0 else VBF_O, 1024)
                            dst = B2(KROT_O if s_ == 0 else VROT_O, 1024)
                            pss = psA[:, 0:1024] if s_ == 0 else psA[:, 1024:2048]
                            ins = v.tensor_tensor(B2(T1B_O, 1024), pss, tab_sin(j), ALU.mult)
                            vinc(ins, ("t1", p, step))
                            v.wait_ge(S["am_evac"], F_evac(p, step + 1))
                            v.tensor_tensor(dst, src, tab_cos(j), ALU.mult)
                            ins = v.tensor_tensor(dst, dst, B2(T1B_O, 1024), ALU.add)
                            vinc(ins, ("rc", p, step))
                    for s_, nko in ((0, NK2_O), (1, NV2_O)):
                        step = 2 * j + s_
                        v.wait_ge(S["am_ksq"], F_ksq(p, step + 1))
                        if j == 0:
                            ins = v.tensor_copy(B2(nko, 1024), B2(KSQ_O, 1024))
                        else:
                            ins = v.tensor_tensor(B2(nko, 1024), B2(KSQ_O, 1024), B2(nko, 1024), ALU.add)
                        vinc(ins, ("nk", p, step))
                    if j >= 2:
                        v.wait_ge(S["gm_pe"], 16 * (16 * p + j - 1))
                    ins = v.tensor_tensor(B2(PBUF_O + (j % 2) * 1024, 1024),
                                          B2(KROT_O, 1024), B2(VROT_O, 1024), ALU.mult)
                    vinc(ins, ("p", p, j))
                    if p == 0 and 3 <= j <= 10:
                        for h2 in range(2):
                            kb = 2 * (j - 3) + h2
                            v.wait_ge(S["sm_x"], 16 * (16 + kb + 1))
                            colb = C_S1A if kb % 2 == 0 else C_S1B
                            ins = v.tensor_reduce(
                                Fv(COL(colb), 64)[:, 8 + kb // 2:9 + kb // 2],
                                Fv(XST_O + (kb % 2) * 1024, 1024), AX.X, ALU.add)
                            vinc(ins, ("s1", 1, kb))
                    if p == 1 and j >= 8:
                        for h in range(2):
                            k = 2 * (j - 8) + h
                            v.wait_ge(S["gm_ff"], 16 * (32 + k + 1))
                            fw = Fv(FFWR_O + (k % 2) * 1024, 1024)
                            v.tensor_scalar(B2(M1B_O, 1024), fw, Fv(THR_O, 8)[:, 0:1], None, ALU.is_gt)
                            ins = v.scalar_tensor_tensor(
                                B2(WCAT_C(k // 2) + 1024 * (k % 2), 1024), fw,
                                Fv(THR_O + 8, 8)[:, 0:1], B2(M1B_O, 1024), ALU.is_ge, ALU.add)
                            vinc(ins, ("ffq", k))
                v.wait_ge(S["pm_nm"], 2 * p + 1)
                ins = v.tensor_copy(Fv(NRMA_O, 1024), psA[:, 0:1024])
                vinc(ins, ("nkc", p))
                v.wait_ge(S["pm_nm"], 2 * p + 2)
                ins = v.tensor_tensor(Fv(NRMB_O, 1024), Fv(NRMA_O, 1024), psA[:, 1024:2048], ALU.mult)
                vinc(ins, ("prod", p))
                v.wait_ge(S["am_sp"], p + 1)
                v.reciprocal(Fv(NRMB_O, 1024), Fv(NRMA_O, 1024))
                ins = v.tensor_copy(B2(WBB_O, 1024), Fv(NRMB_O, 1024))
                vinc(ins, ("wb", p))
                PRET4 = [PRET_O, PRET_O + 1024, KROT_O, VROT_O]
                FFW4 = [FFWR_O, FFWR_O + 1024, XST_O, XST_O + 1024]
                for j in range(NPAIR):
                    v.wait_ge(S["gm_pr"], 16 * (16 * p + j + 1))
                    ins = v.scalar_tensor_tensor(
                        B2(ADUMP_O, 1024), B2(PRET4[j % 4], 1024),
                        0.0, B2(WBB_O, 1024), ALU.bypass, ALU.mult,
                        accum_out=Fv(APART_O, 32)[:, 16 * p + j:16 * p + j + 1])
                    vinc(ins, ("ap", p, j))
                    if p == 1:
                        k = 16 + j
                        v.wait_ge(S["gm_ff"], 16 * (32 + k + 1))
                        fw = Fv(FFW4[k % 4], 1024)
                        v.tensor_scalar(B2(M1B_O, 1024), fw, Fv(THR_O, 8)[:, 0:1], None, ALU.is_gt)
                        ins = v.scalar_tensor_tensor(
                            B2(WCAT_C(k // 2) + 1024 * (k % 2), 1024), fw,
                            Fv(THR_O + 8, 8)[:, 0:1], B2(M1B_O, 1024), ALU.is_ge, ALU.add)
                        vinc(ins, ("ffq", k))
                if p == 1:
                    v.tensor_tensor(Fv(A2_O, 16), Fv(APART_O, 32)[:, 0:16],
                                    Fv(APART_O, 32)[:, 16:32], ALU.add)
                    v.memset(Fv(ABTEV_O, 64), 0.0)
                    v.memset(Fv(ABTEV_O, 64), 0.0)
                    v.wait_ge(S["sm_par"], 16 * 9)
                    for g_ in range(4):
                        ins = v.tensor_tensor(Fv(FA_O + 16 * g_, 16), Fv(A2_O, 16),
                                              Fv(BT16_O + 16 * g_, 16), ALU.mult)
                    vinc(ins, ("a2",))
                    v.wait_ge(S["gm_misc"], 16 * 8)
                    v.tensor_tensor(Fv(FA_O, 64), Fv(FA_O, 64), Fv(BT16_O, 64), ALU.mult)
                    v.memset(Fv(ABTIN_O, 64), 0.0)
                    v.memset(Fv(ABTEV_O, 64), 0.0)
                    v.tensor_tensor(Fv(ABTIN_O, 32), Fv(FA_O, 32), Fv(FA_O + 32, 32), ALU.add)
                    v.memset(Fv(A2_O, 16), 0.0)
                    v.memset(Fv(ABTEV_O, 64), 0.0)
                    v.tensor_tensor(Fv(A2_O, 16), Fv(ABTIN_O, 16), Fv(ABTIN_O + 16, 16), ALU.add)
                    v.memset(Fv(ABTEV_O, 64), 0.0)
                    v.memset(Fv(ABTEV_O, 64), 0.0)
                    ins = v.tensor_scalar(Fv(ACOL_O, 16), Fv(A2_O, 16), 0.5, 0.5, ALU.mult, ALU.add)
                    vinc(ins, ("af",))
                bar(v, "b_ve", 3 + 2 * p)
            ins = v.memset(Fv(BIAST_O, 2048), 0.0)
            vinc(ins, ("btz",))
            v.wait_ge(S["pm_bc"], 8)
            ins = v.tensor_copy(Fv(BIAST_O, 2048), psB[:, 0:2048])
            vinc(ins, ("bT",))
            for stt in range(NST):
                for h in range(2):
                    k = 2 * stt + h
                    v.wait_ge(S["sm_x2"], 16 * (k + 1))
                    if k >= 2:
                        v.wait_ge(S["sm_out"], 16 * (k - 1))
                    ob = Fv(OUTR_O + (k % 2) * 1024, 1024)
                    ins = v.scalar_tensor_tensor(
                        ob, Fv(X2R_O + (k % 2) * 1024, 1024),
                        Fv(COL(C_C1), 64)[:, stt:stt + 1],
                        Fv(BIAST_O, 2048)[:, 1024 * h:1024 * h + 1024], ALU.mult, ALU.add)
                    vinc(ins, ("w1", k))
                    v.wait_ge(S["pm_ph2"], stt + 1)
                    ins = v.scalar_tensor_tensor(
                        ob, ob, Fv(COL(C_C2), 64)[:, stt:stt + 1],
                        (psA if stt % 2 == 0 else psB)[:, 1024 * h:1024 * h + 1024],
                        ALU.add, ALU.add)
                    vinc(ins, ("o", k))
            bar(v, "b_ve", 6)

        @block.scalar
        def _(s):
            nAff = [0]
            nEv = [0]
            nKq = [0]
            bar(s, "b_ac", 1)
            s.wait_ge(S["sm_par"], 16 * 8)
            for p in range(2):
                if p == 0:
                    for k in range(16):
                        t, h = k // 2, k % 2
                        s.wait_ge(S["sm_x"], 16 * (k + 1))
                        col = C_S2A if h == 0 else C_S2B
                        s.activation(B2(DUMPS_O, 1024), Fv(XST_O + (k % 2) * 1024, 1024),
                                     AF.Square,
                                     accum_out=Fv(COL(col), 64)[:, t:t + 1]).then_inc(S["am_st"], 1)
                s.wait_ge(S["vm"], M[("rm1", p)])
                s.activation(Fv(COL(C_T1), 64), Fv(COL(C_A1), 64), AF.Sqrt).then_inc(S["am_sq"], 1)
                s.wait_ge(S["vm"], M[("rm2", p)])
                s.activation(Fv(COL(C_T2), 64), Fv(COL(C_B2), 64), AF.Sqrt).then_inc(S["am_sq"], 1)
                for c in range(NCH):
                    s.wait_ge(S["vm"], M[("ut", p, c)])
                    xb_ = Fv(XTB_O + (c % 2) * 1024, 1024) if p == 0 else \
                        Fv([XTB_O, XTB_O + 1024, FFWR_O, FFWR_O + 1024][c % 4], 1024)
                    s.activation(B3(XN2_O, 16, 1024)[:, c, :], xb_, AF.Identity,
                                 bias=Fv(BCOL_O, 16)[:, c:c + 1],
                                 scale=Fv(GCOL_O, 16)[:, c:c + 1]).then_inc(S["am_aff"], 1)
                    if c >= 4:
                        s.wait_ge(S["sm_fe"], 16 * 4 * (16 * p + c - 3))
                        s.wait_ge(S["sm_fev"], 16 * 2 * (16 * p + c - 3))
                        s.wait_ge(S["sm_feg"], 16 * 2 * (16 * p + c - 3))
                    s.activation(B2(FINB_O + (c % 4) * 1024, 1024), xb_, AF.Identity,
                                 bias=Fv(LNB_O, 16)[:, c:c + 1],
                                 scale=Fv(LNG_O, 16)[:, c:c + 1]).then_inc(S["am_aff"], 1)
                    nAff[0] += 2
                    assert nAff[0] == F_aff(p, 2 * c + 2)
                    if c >= 2:
                        for t in (4, 5):
                            s.dma_start(out=fh[p][t, :, c - 2, :],
                                        in_=B2(FINB_O + ((c - 2) % 4) * 1024, 1024)
                                        [:, 128 * t:128 * (t + 1)]).then_inc(S["sm_fev"], 16)
                for ct in (14, 15):
                    for t in (4, 5):
                        s.dma_start(out=fh[p][t, :, ct, :],
                                    in_=B2(FINB_O + (ct % 4) * 1024, 1024)
                                    [:, 128 * t:128 * (t + 1)]).then_inc(S["sm_fev"], 16)
                bar(s, "b_ac", 2 + 2 * p)
                for j in range(NPAIR):
                    for s_ in range(2):
                        step = 2 * j + s_
                        q = (16 if s_ == 0 else 32) + j
                        s.wait_ge(S["pm_mm"], F_mm(p, step + 1))
                        if j < 8:
                            dst = B2(KBF_O if s_ == 0 else VBF_O, 1024)
                            if j >= 1:
                                s.wait_ge(S["vm"], M[("rc", p, 2 * (j - 1) + s_)])
                        else:
                            dst = B2(KROT_O if s_ == 0 else VROT_O, 1024)
                            s.wait_ge(S["vm"], M[("p", p, j - 1)])
                        pss = psA[:, 0:1024] if s_ == 0 else psA[:, 1024:2048]
                        s.activation(dst, pss, AF.Identity,
                                     bias=Fv(BIN_O, 48)[:, q:q + 1]).then_inc(S["am_evac"], 1)
                        nEv[0] += 1
                        assert nEv[0] == F_evac(p, step + 1)
                    for s_ in range(2):
                        step = 2 * j + s_
                        src = B2(KROT_O if s_ == 0 else VROT_O, 1024)
                        if j < 8:
                            s.wait_ge(S["vm"], M[("rc", p, step)])
                        if s_ == 1:
                            s.wait_ge(S["vm"], M[("nk", p, 2 * j)])
                        s.activation(B2(KSQ_O, 1024), src, AF.Square).then_inc(S["am_ksq"], 1)
                        nKq[0] += 1
                        assert nKq[0] == F_ksq(p, step + 1)
                    if p == 0 and 3 <= j <= 10:
                        for h2 in range(2):
                            kb = 2 * (j - 3) + h2
                            s.wait_ge(S["sm_x"], 16 * (16 + kb + 1))
                            colb = C_S2A if kb % 2 == 0 else C_S2B
                            s.activation(B2(MB_O, 1024), Fv(XST_O + (kb % 2) * 1024, 1024),
                                         AF.Square,
                                         accum_out=Fv(COL(colb), 64)[:, 8 + kb // 2:9 + kb // 2]
                                         ).then_inc(S["am_st"], 1)
                    if p == 1 and j >= 8:
                        for h in range(2):
                            k = 2 * (j - 8) + h
                            s.wait_ge(S["vm"], M[("ffq", k)])
                            wslot = B2(WCAT_C(k // 2) + 1024 * (k % 2), 1024)
                            s.activation(wslot, wslot, AF.Identity,
                                         bias=Fv(THR_O + 16, 8)[:, 0:1]).then_inc(S["am_ffq"], 1)
                s.wait_ge(S["vm"], M[("prod", p)])
                s.activation(Fv(NRMA_O, 1024), Fv(NRMB_O, 1024), AF.Sqrt).then_inc(S["am_sp"], 1)
                if p == 1:
                    pass
                for k in range(NQ):
                    step = 32 + k
                    s.wait_ge(S["pm_mm"], F_mm(p, step + 1))
                    if k >= 2:
                        s.wait_ge(S["sm_qe"], 16 * 8 * (16 * p + k - 1))
                    pss = psA[:, 0:1024] if step % 2 == 0 else psA[:, 1024:2048]
                    s.activation(B2(QBF_O + (k % 2) * 1024, 1024), pss, AF.Identity,
                                 bias=Fv(BIN_O, 48)[:, k:k + 1]).then_inc(S["am_evac"], 1)
                    nEv[0] += 1
                if p == 1:
                    for kf in range(16, 32):
                        s.wait_ge(S["vm"], M[("ffq", kf)])
                        wslot = B2(WCAT_C(kf // 2) + 1024 * (kf % 2), 1024)
                        s.activation(wslot, wslot, AF.Identity,
                                     bias=Fv(THR_O + 16, 8)[:, 0:1]).then_inc(S["am_ffq"], 1)
                if p == 1:
                    s.wait_ge(S["vm"], M[("af",)])
                    for idx, j in enumerate(range(10, 16)):
                        s.wait_ge(S["gm_wc"], 16 * (idx + 1))
                        s.activation(B2(WCAT_C(16 + j), 2048), B2(WCAT_C(16 + j), 2048),
                                     AF.Identity,
                                     scale=Fv(ACOL_O, 16)[:, j:j + 1]).then_inc(S["am_ffq"], 1)
                bar(s, "b_ac", 3 + 2 * p)
            for stt in range(NST):
                for h in range(2):
                    k = 2 * stt + h
                    if k >= 2:
                        s.wait_ge(S["vm"], M[("w1", k - 2)])
                    s.dma_start(out=Fv(X2R_O + (k % 2) * 1024, 1024),
                                in_=x_d[128 * stt:128 * (stt + 1), 1024 * h:1024 * h + 1024]
                                ).then_inc(S["sm_x2"], 16)
                if stt == 0:
                    for idx, j in enumerate(range(10)):
                        s.wait_ge(S["gm_wc"], 16 * (7 + idx))
                        s.activation(B2(WCAT_C(16 + j), 2048), B2(WCAT_C(16 + j), 2048),
                                     AF.Identity,
                                     scale=Fv(ACOL_O, 16)[:, j:j + 1]).then_inc(S["am_ffq"], 1)
                if stt >= 1:
                    for h in range(2):
                        k = 2 * (stt - 1) + h
                        s.wait_ge(S["vm"], M[("o", k)])
                        s.dma_start(out=out_d[128 * (stt - 1):128 * stt, 1024 * h:1024 * h + 1024],
                                    in_=Fv(OUTR_O + (k % 2) * 1024, 1024)).then_inc(S["sm_out"], 16)
            for h in range(2):
                k = 30 + h
                s.wait_ge(S["vm"], M[("o", k)])
                s.dma_start(out=out_d[128 * 15:2048, 1024 * h:1024 * h + 1024],
                            in_=Fv(OUTR_O + (k % 2) * 1024, 1024)).then_inc(S["sm_out"], 16)
            bar(s, "b_ac", 6)

        @block.tensor
        def _(te):
            pmm = [0]
            pbc = [0]

            def bcinc(ins):
                ins.then_inc(S["pm_bc"], 1)
                pbc[0] += 1
                return pbc[0]

            bar(te, "b_pe", 1)
            te.wait_ge(S["sm_par"], 16 * 8)
            for p in range(2):
                if p == 0:
                    te.wait_ge(S["vm"], M[("facol",)])
                    assert bcinc(te.matmul(psB[:, 1024:1536], Fv(ONESF_O, 128),
                                           Fv(XST_O, 512),
                                           start=True, stop=True)) == 1
                te.wait_ge(S["vm"], M[("abpack", p)])
                assert bcinc(te.transpose(psB[:, 0:128], Fv(ABTIN_O, 128),
                                          Fv(IDENT_O, 128))) == (2 if p == 0 else 5)
                te.wait_ge(S["gm_misc"], 16 * (3 + 3 * p))
                for half in range(2):
                    ins = te.matmul(psB[:, 512 * half:512 * half + 512], Fv(ONESF_O, 128),
                                    Fv(AROW_O, 1024)[:, 512 * half:512 * half + 512],
                                    start=True, stop=True)
                assert bcinc(ins) == (3 if p == 0 else 6)
                for half in range(2):
                    ins = te.matmul(psB[:, 1024 + 512 * half:1024 + 512 * half + 512],
                                    Fv(ONESF_O, 128),
                                    Fv(BROW_O, 1024)[:, 512 * half:512 * half + 512],
                                    start=True, stop=True)
                assert bcinc(ins) == (4 if p == 0 else 7)
                bar(te, "b_pe", 2 + 2 * p)
                for i in range(48):
                    te.wait_ge(S["gm_w"], 16 * (48 * p + i + 1))
                    if i >= 2:
                        prev = i - 2
                        if prev < 16:
                            te.wait_ge(S["vm"], M[("t1", p, prev)])
                        elif i == 32:
                            te.wait_ge(S["vm"], M[("nkc", p)])
                        elif i == 33:
                            te.wait_ge(S["vm"], M[("prod", p)])
                        else:
                            te.wait_ge(S["am_evac"], F_evac(p, prev + 1))
                    pss = psA[:, 0:1024] if i % 2 == 0 else psA[:, 1024:2048]
                    wsl = B3(WSLAB_O + (i % 2) * 2048, 16, 128)
                    xn2 = B3(XN2_O, 16, 1024)
                    for c in range(NCH):
                        for g_ in range(2):
                            ins = te.matmul(pss[:, 512 * g_:512 * g_ + 512], wsl[:, c, :],
                                            xn2[:, c, 512 * g_:512 * g_ + 512],
                                            start=(c == 0), stop=(c == NCH - 1))
                    ins.then_inc(S["pm_mm"], 1)
                    pmm[0] += 1
                    assert pmm[0] == F_mm(p, i + 1)
                    if i < 32 and i % 2 == 1 and i // 2 < 8:
                        j = i // 2
                        for s_ in range(2):
                            step = 2 * j + s_
                            te.wait_ge(S["am_evac"], F_evac(p, step + 1))
                            src = B2(KBF_O if s_ == 0 else VBF_O, 1024)
                            pss2 = psA[:, 0:1024] if s_ == 0 else psA[:, 1024:2048]
                            for g_ in range(2):
                                ins = te.matmul(pss2[:, 512 * g_:512 * g_ + 512], B2(PSW_O, 128),
                                                src[:, 512 * g_:512 * g_ + 512],
                                                start=True, stop=True)
                            ins.then_inc(S["pm_rh"], 1)
                    if i == 31:
                        te.wait_ge(S["vm"], M[("nk", p, 31)])
                        te.wait_ge(S["am_evac"], F_evac(p, 31))
                        for g_ in range(2):
                            ins = te.matmul(psA[:, 512 * g_:512 * g_ + 512], B2(ONESB_O, 128),
                                            B2(NK2_O, 1024)[:, 512 * g_:512 * g_ + 512],
                                            start=True, stop=True)
                        ins.then_inc(S["pm_nm"], 1)
                        te.wait_ge(S["am_evac"], F_evac(p, 32))
                        for g_ in range(2):
                            ins = te.matmul(psA[:, 1024 + 512 * g_:1024 + 512 * g_ + 512],
                                            B2(ONESB_O, 128),
                                            B2(NV2_O, 1024)[:, 512 * g_:512 * g_ + 512],
                                            start=True, stop=True)
                        ins.then_inc(S["pm_nm"], 1)
                bar(te, "b_pe", 3 + 2 * p)
            te.wait_ge(S["sm_par"], 16 * 11)
            for g_ in range(4):
                ins = te.matmul(psB[:, 512 * g_:512 * g_ + 512], Fv(ONESF_O, 128),
                                Fv(BIAST_O, 2048)[:, 512 * g_:512 * g_ + 512],
                                start=True, stop=True)
            assert bcinc(ins) == 8
            d_seq = list(range(16)) + list(range(26, 32)) + list(range(16, 26))
            wc_idx = {26 + j: j + 1 for j in range(6)}
            wc_idx.update({16 + j: 7 + j for j in range(10)})
            for stt in range(NST):
                te.wait_ge(S["sm_sr"], 16 * (2 * stt + 2))
                if stt >= 2:
                    te.wait_ge(S["vm"], M[("o", 2 * (stt - 2) + 1)])
                if stt < 2:
                    te.wait_ge(S["vm"], M[("bT",)])
                pss = psA if stt % 2 == 0 else psB
                fsl = B3(SLABF_O + (stt % 4) * 2048, 16, 128)
                ysl = B3(SLABQ_O + (stt % 4) * 2048, 16, 128)
                if stt == 1:
                    te.wait_ge(S["am_ffq"], 48)
                for di, d in enumerate(d_seq):
                    if d >= 16 and stt == 0:
                        te.wait_ge(S["am_ffq"], 32 + wc_idx[d])
                    statn = fsl[:, d, :] if d < 16 else ysl[:, d - 16, :]
                    for g_ in range(4):
                        ins = te.matmul(pss[:, 512 * g_:512 * g_ + 512], statn,
                                        B2(WCAT_C(d), 2048)[:, 512 * g_:512 * g_ + 512],
                                        start=(di == 0), stop=(di == 31))
                ins.then_inc(S["pm_ph2"], 1)
            bar(te, "b_pe", 6)

        @block.gpsimd
        def _(g):
            gmisc = [0]

            def gmi(ins):
                ins.then_inc(S["gm_misc"], 16)
                gmisc[0] += 1
                return gmisc[0]

            bar(g, "b_gp", 1)
            for p in range(2):
                if p == 0:
                    for k in range(32):
                        if k >= 2:
                            g.wait_ge(S["vm"], M[("fa", k - 2)])
                        g.dma_start(out=Fv(FFWR_O + (k % 2) * 1024, 1024),
                                    in_=ffw_d[k // 2][:, 1024 * (k % 2):1024 * (k % 2) + 1024]
                                    ).then_inc(S["gm_ff"], 16)
                g.wait_ge(S["vm"], M[("abev", p)])
                g.wait_ge(S["vm"], M[("abz", p)])
                assert gmi(g.dma_start(out=abd, in_=Fv(ABTEV_O, 128)[0:16, :])) == 1 + 3 * p
                g.wait_ge(S["gm_misc"], 16 * (1 + 3 * p))
                assert gmi(g.dma_start(out=Fv(AROW_O, 1024)[0:1, :],
                                       in_=abd[0:8].rearrange("c p -> (c p)")[None, :])) == 2 + 3 * p
                assert gmi(g.dma_start(out=Fv(BROW_O, 1024)[0:1, :],
                                       in_=abd[8:16].rearrange("c p -> (c p)")[None, :])) == 3 + 3 * p
                def feg_g(c):
                    g.wait_ge(S["am_aff"], F_aff(p, 2 * c + 2))
                    for t in (6, 7):
                        g.dma_start(out=fh[p][t, :, c, :],
                                    in_=B2(FINB_O + (c % 4) * 1024, 1024)[:, 128 * t:128 * (t + 1)]
                                    ).then_inc(S["sm_feg"], 16)

                for c in range(NCH):
                    if p == 0:
                        if c >= 2:
                            g.wait_ge(S["am_aff"], F_aff(p, 2 * (c - 1)))
                        g.dma_start(out=Fv(XTB_O + (c % 2) * 1024, 1024),
                                    in_=xt_d[c][:, 0:TH]).then_inc(S["gm_xt"], 16)
                    elif c >= 4:
                        g.wait_ge(S["am_aff"], F_aff(1, 2 * (c - 4) + 2))
                        g.dma_start(out=Fv([XTB_O, XTB_O + 1024, FFWR_O,
                                            FFWR_O + 1024][c % 4], 1024),
                                    in_=xt_d[c][:, TH:2 * TH]).then_inc(S["gm_xt"], 16)
                    if c >= 3:
                        feg_g(c - 3)
                for ct in (13, 14, 15):
                    feg_g(ct)
                for i0 in range(2):
                    g.dma_start(out=B3(WSLAB_O + (i0 % 2) * 2048, 16, 128),
                                in_=win_d[(16, 32)[i0]]).then_inc(S["gm_w"], 16)
                bar(g, "b_gp", 2 + 2 * p)
                fo_list = []
                for j in range(NPAIR):
                    fo_list += [16 + j, 32 + j]
                fo_list += list(range(16))

                def wload(i):
                    if i >= 2:
                        g.wait_ge(S["pm_mm"], F_mm(p, i - 1))
                    g.dma_start(out=B3(WSLAB_O + (i % 2) * 2048, 16, 128),
                                in_=win_d[fo_list[i]]).then_inc(S["gm_w"], 16)

                for i in range(48):
                    if i + 2 < 48:
                        wload(i + 2)
                    if i < 32 and i % 2 == 1:
                        j = i // 2
                        if p == 1 and 7 <= j <= 14:
                            for h in range(2):
                                k = 2 * (j - 7) + h
                                if k >= 2:
                                    g.wait_ge(S["vm"], M[("ffq", k - 2)])
                                g.dma_start(out=Fv(FFWR_O + (k % 2) * 1024, 1024),
                                            in_=ffw_d[k // 2][:, 1024 * (k % 2):1024 * (k % 2) + 1024]
                                            ).then_inc(S["gm_ff"], 16)
                        g.wait_ge(S["vm"], M[("p", p, j)])
                        g.dma_start(out=ph[p][j], in_=B2(PBUF_O + (j % 2) * 1024, 1024)
                                    ).then_inc(S["gm_pe"], 16)
                    if i >= 32:
                        jr = i - 32
                        PRET4 = [PRET_O, PRET_O + 1024, KROT_O, VROT_O]
                        FFW4 = [FFWR_O, FFWR_O + 1024, XST_O, XST_O + 1024]
                        if p == 1:
                            k = 16 + jr
                            if k - 4 >= 16:
                                g.wait_ge(S["vm"], M[("ffq", k - 4)])
                            else:
                                g.wait_ge(S["vm"], M[("ffq", k - 2)])
                            if FFW4[k % 4] in (XST_O, XST_O + 1024) and k < 20:
                                g.wait_ge(S["vm"], M[("wb", p)])
                            g.dma_start(out=Fv(FFW4[k % 4], 1024),
                                        in_=ffw_d[k // 2][:, 1024 * (k % 2):1024 * (k % 2) + 1024]
                                        ).then_inc(S["gm_ff"], 16)
                        g.wait_ge(S["gm_pe"], 16 * (16 * p + jr + 1))
                        if jr >= 4:
                            g.wait_ge(S["vm"], M[("ap", p, jr - 4)])
                        if jr in (2, 3):
                            g.wait_ge(S["vm"], M[("p", p, 15)])
                        g.dma_start(out=B2(PRET4[jr % 4], 1024), in_=ph[p][jr]
                                    ).then_inc(S["gm_pr"], 16)
                if p == 0:
                    for j in range(10, 16):
                        g.dma_start(out=B2(WCAT_C(16 + j), 2048), in_=wout_d[j]
                                    ).then_inc(S["gm_wc"], 16)
                    for c in range(2):
                        g.dma_start(out=Fv(XTB_O + c * 1024, 1024),
                                    in_=xt_d[c][:, TH:2 * TH]).then_inc(S["gm_xt"], 16)
                    g.wait_ge(S["vm"], M[("fa", 31)])
                    for c in range(2, 4):
                        g.dma_start(out=Fv(FFWR_O + (c - 2) * 1024, 1024),
                                    in_=xt_d[c][:, TH:2 * TH]).then_inc(S["gm_xt"], 16)
                if p == 1:
                    g.wait_ge(S["vm"], M[("a2",)])
                    assert gmi(g.dma_start(out=ar_in, in_=Fv(FA_O, 64))) == 7
                    g.wait_ge(S["gm_misc"], 16 * 7)
                    g.collective_compute(
                        "AllReduce", ALU.add,
                        replica_groups=[[0, 1, 2, 3, 4, 5, 6, 7]],
                        ins=[ar_in[:, :]], outs=[ar_out[:, :]],
                    ).then_inc(S["cc"], 1)
                    g.wait_ge(S["cc"], 1)
                    assert gmi(g.dma_start(out=Fv(FA_O, 64), in_=ar_out)) == 8
                bar(g, "b_gp", 3 + 2 * p)
            g.wait_ge(S["vm"], M[("ap", 1, 15)])
            g.wait_ge(S["vm"], M[("ffq", 31)])
            g.wait_ge(S["gm_pe"], 16 * 32)
            for d in range(10):
                g.dma_start(out=B2(WCAT_C(16 + d), 2048), in_=wout_d[d]
                            ).then_inc(S["gm_wc"], 16)
            bar(g, "b_gp", 6)

        @block.sync
        def _(sy):
            for dst, src in ((Fv(GCOL_O, 16), gcol_d), (Fv(BCOL_O, 16), bcol_d),
                             (Fv(LNG_O, 16), lng_d), (Fv(LNB_O, 16), lnb_d),
                             (Fv(BIN_O, 48), bin_d), (B2(PSW_O, 128), psw_d),
                             (B2(ONESB_O, 128), onesb_d), (Fv(IDENT_O, 128), ident_d),
                             (Fv(ABTEV_O + 128, 64) if False else None, None)):
                if dst is None:
                    continue
                sy.dma_start(out=dst, in_=src).then_inc(S["sm_par"], 16)
            sy.dma_start(out=Fv(BT16_O, 64), in_=armask_d).then_inc(S["sm_par"], 16)
            bar(sy, "b_sy", 1)
            def xstload(p_, k):
                gt, h = 8 * p_ + k // 2, k % 2
                sy.dma_start(out=Fv(XST_O + (k % 2) * 1024, 1024),
                             in_=x_d[128 * gt:128 * (gt + 1), 1024 * h:1024 * h + 1024]
                             ).then_inc(S["sm_x"], 16)

            for p in range(2):
                if p == 0:
                    for k in range(16):
                        if k >= 2:
                            sy.wait_ge(S["vm"], M[("s1", 0, k - 2)])
                            sy.wait_ge(S["am_st"], k - 1)
                        xstload(0, k)
                for c in range(NCH):
                    sy.wait_ge(S["am_aff"], F_aff(p, 2 * c + 2))
                    for t in range(4):
                        sy.dma_start(out=fh[p][t, :, c, :],
                                     in_=B2(FINB_O + (c % 4) * 1024, 1024)[:, 128 * t:128 * (t + 1)]
                                     ).then_inc(S["sm_fe"], 16)
                bar(sy, "b_sy", 2 + 2 * p)
                sy.wait_ge(S["sm_fe"], 16 * 4 * 16 * (p + 1))
                sy.wait_ge(S["sm_fev"], 16 * 2 * 16 * (p + 1))
                sy.wait_ge(S["sm_feg"], 16 * 2 * 16 * (p + 1))
                for j in range(8):
                    if j >= 2:
                        sy.wait_ge(S["vm"], M[("rc", p, 2 * (j - 2) + 1)])
                    sy.dma_start(out=tab_cos(j), in_=cost_d[p][j]).then_inc(S["sm_tab"], 16)
                    sy.dma_start(out=tab_sin(j), in_=sint_d[p][j]).then_inc(S["sm_tab"], 16)
                    if p == 0 and j == 1:
                        xstload(1, 0)
                        xstload(1, 1)
                    if p == 0 and j >= 5:
                        for kb in (2 * (j - 4), 2 * (j - 4) + 1):
                            sy.wait_ge(S["vm"], M[("s1", 1, kb - 2)])
                            sy.wait_ge(S["am_st"], 16 + kb - 1)
                            xstload(1, kb)
                if p == 0:
                    for kb in range(8, 16):
                        sy.wait_ge(S["vm"], M[("s1", 1, kb - 2)])
                        sy.wait_ge(S["am_st"], 16 + kb - 1)
                        xstload(1, kb)
                for k in range(NQ):
                    sy.wait_ge(S["am_evac"], F_evac(p, 32 + k + 1))
                    for t in range(8):
                        sy.dma_start(out=qh[p][t, :, k, :],
                                     in_=B2(QBF_O + (k % 2) * 1024, 1024)[:, 128 * t:128 * (t + 1)]
                                     ).then_inc(S["sm_qe"], 16)
                bar(sy, "b_sy", 3 + 2 * p)
            sy.wait_ge(S["vm"], M[("btz",)])
            sy.dma_start(out=Fv(BIAST_O, 2048)[0:1, :], in_=boutr_d).then_inc(S["sm_par"], 16)
            sy.dma_start(out=Fv(BIAST_O, 2048)[1:2, :], in_=fbr_d).then_inc(S["sm_par"], 16)
            sy.wait_ge(S["sm_fe"], 16 * 4 * 32)
            sy.wait_ge(S["sm_fev"], 16 * 2 * 32)
            sy.wait_ge(S["sm_feg"], 16 * 2 * 32)
            sy.wait_ge(S["sm_qe"], 16 * 8 * 32)
            for stt in range(NST):
                p, stl = stt // 8, stt % 8
                if stt >= 4:
                    sy.wait_ge(S["pm_ph2"], stt - 3)
                sy.dma_start(out=B3(SLABF_O + (stt % 4) * 2048, 16, 128),
                             in_=fh[p][stl]).then_inc(S["sm_sr"], 16)
                sy.dma_start(out=B3(SLABQ_O + (stt % 4) * 2048, 16, 128),
                             in_=qh[p][stl]).then_inc(S["sm_sr"], 16)
            bar(sy, "b_sy", 6)
            sy.wait_ge(S["sm_out"], 16 * 32)

    es.close()
    return nc


_NC_CACHE = [None]


def kernel(L0_attn_X, L0_attn_W_in, L0_attn_b_in, L0_attn_W_out, L0_attn_b_out,
           L0_attn_gamma, L0_attn_beta, L0_ffn_weight, L0_ffn_bias,
           L0_ffn_ln_gamma, L0_ffn_ln_beta):
    X = np.ascontiguousarray(np.asarray(L0_attn_X, dtype=np.float32)).reshape(16384, 2048)
    W_in = np.asarray(L0_attn_W_in, dtype=np.float32)
    W_out = np.asarray(L0_attn_W_out, dtype=np.float32)
    ffw = np.asarray(L0_ffn_weight, dtype=np.float32)

    win_r = np.ascontiguousarray(
        W_in.reshape(16, 128, 48, 128).transpose(2, 1, 0, 3))
    wout_r = np.ascontiguousarray(W_out.reshape(16, 128, 2048))
    ffwT = np.ascontiguousarray(ffw.T).reshape(16, 128, 2048)

    def colform(vv, w):
        return np.ascontiguousarray(np.asarray(vv, dtype=np.float32).reshape(w, 128).T)

    gcol = colform(L0_attn_gamma, 16)
    bcol = colform(L0_attn_beta, 16)
    lngcol = colform(L0_ffn_ln_gamma, 16)
    lnbcol = colform(L0_ffn_ln_beta, 16)
    bincol = colform(L0_attn_b_in, 48)
    boutrow = np.ascontiguousarray(np.asarray(L0_attn_b_out, dtype=np.float32).reshape(1, 2048))
    fbrow = np.ascontiguousarray(np.asarray(L0_ffn_bias, dtype=np.float32).reshape(1, 2048))

    rot = 1024
    inv = (np.float32(1.0) / (np.float32(10000.0) **
                              (np.arange(0, rot, 2, dtype=np.float32) / np.float32(rot)))
           ).astype(np.float32)
    pos = np.arange(4096, dtype=np.float32)
    fr = np.repeat((pos[:, None] * inv[None, :]).astype(np.float32), 2, axis=1)
    cosf = np.cos(fr).astype(np.float32).T
    sinf = np.sin(fr).astype(np.float32).T

    psw = np.zeros((128, 128), dtype=np.float32)
    for i in range(64):
        psw[2 * i + 1, 2 * i] = -1.0
        psw[2 * i, 2 * i + 1] = 1.0
    psw_bf = psw.astype(BFH)
    onesb = np.ones((128, 128), dtype=BFH)
    armasks = []
    for c in range(NCORE):
        m_ = np.zeros((128, 64), dtype=np.float32)
        m_[:, 16 * (c // 2):16 * (c // 2) + 16] = 1.0
        armasks.append(m_)
    ident = np.eye(128, dtype=np.float32)

    if _NC_CACHE[0] is None:
        _NC_CACHE[0] = build_nc()
    nc = _NC_CACHE[0]

    in_maps = []
    for c in range(NCORE):
        b, half = c // 2, c % 2
        r0 = b * 4096 + half * 2048
        xc = np.ascontiguousarray(X[r0:r0 + 2048])
        xtc = np.ascontiguousarray(xc.T).reshape(16, 128, 2048)
        s0 = half * 2048
        ctp = np.ascontiguousarray(
            cosf[:, s0:s0 + 2048].reshape(8, 128, 2, 1024).transpose(2, 0, 1, 3)).astype(BFH)
        stp = np.ascontiguousarray(
            sinf[:, s0:s0 + 2048].reshape(8, 128, 2, 1024).transpose(2, 0, 1, 3)).astype(BFH)
        in_maps.append({
            "x": xc, "xt": xtc, "win": win_r, "wout": wout_r, "ffw": ffwT,
            "cost": ctp, "sint": stp,
            "gcol": gcol, "bcol": bcol, "lngcol": lngcol, "lnbcol": lnbcol,
            "bincol": bincol, "boutrow": boutrow, "fbrow": fbrow,
            "pairswap": psw_bf, "onesb": onesb, "ident": ident, "armask": armasks[c],
        })

    res = run_bass_kernel_spmd(nc, in_maps, core_ids=list(range(NCORE)))
    out = np.empty((16384, 2048), dtype=np.float32)
    for c in range(NCORE):
        b, half = c // 2, c % 2
        r0 = b * 4096 + half * 2048
        out[r0:r0 + 2048] = res.results[c]["out"]
    return out.reshape(4, 4096, 2048)
